# version 29
# baseline (speedup 1.0000x reference)
import numpy as np
import ml_dtypes
import concourse.bass as bass
import concourse.mybir as mybir
from concourse.bass_utils import run_bass_kernel_spmd
from concourse.dve_op import DveOp
from concourse.dve_spec import Spec, Src0, Src1, C0, C1, C2, C3, sq, lower
import concourse.dve_ops as dve_ops

F32 = mybir.dt.float32
BF16 = mybir.dt.bfloat16
U8 = mybir.dt.uint8
AX = mybir.AxisListType
AOP = mybir.AluOpType
ACTF = mybir.ActivationFunctionType

B, L, D = 8, 2048, 1024
H, DQH, DVH = 4, 128, 256
DK, DV = 512, 1024
I_FFN = 2816
CHUNK = 128
NCH = L // CHUNK
NT = L // 128
EPS_LN = 1e-5
EPS_RMS = 1e-5

def _mk_qlo():
    a = sq(Src0)
    return Spec(body=((a > C0) + (a > C1)) + ((a > C2) + (a > C3)))

def _qlo_ref(in0, in1, s0, s1, imm2):
    a = in0.astype(np.float64) ** 2
    s3 = np.asarray(in1).reshape(in0.shape[0], -1)[:, :1].astype(np.float64)
    return ((a > s0).astype(np.float64) + (a > s1) + (a > imm2) + (a > s3))

def _mk_qhi():
    a = sq(Src0)
    h = ((a > C0) + (a > C1)) + ((a > C2) + (a > C2))
    return Spec(body=h + h)

def _qhi_ref(in0, in1, s0, s1, imm2):
    a = in0.astype(np.float64) ** 2
    return 2.0 * ((a > s0).astype(np.float64) + (a > s1) + 2.0 * (a > imm2))

def _register_op(name, spec, ref, slot):
    for op in dve_ops.OPS:
        if op.name == name:
            return op
    sha = {v: lower(spec, ver=v, perf_en=False, name=name).sha(v) for v in ("v3",)}
    op = DveOp(name, spec, subdim=False, uops_sha=sha, reference=ref)
    old = dve_ops.OPS[slot]
    dve_ops.OPS[slot] = op
    dve_ops._SUB_OPCODE_FOR_NAME[name] = dve_ops._CUSTOM_DVE_ROW_BASE + slot
    dve_ops.CUSTOM_DVE_SPECS.pop(old.name, None)
    dve_ops.CUSTOM_DVE_SPECS[name] = spec
    dve_ops.CUSTOM_DVE_REFS.pop(old.name, None)
    dve_ops.CUSTOM_DVE_REFS[name] = ref
    return op

QLO = _register_op("ANT_QLO", _mk_qlo(), _qlo_ref, len(dve_ops.OPS) - 1)
QHI = _register_op("ANT_QHI", _mk_qhi(), _qhi_ref, len(dve_ops.OPS) - 2)

ENGINES = ("sync", "scalar", "vector", "tensor", "gpsimd")
DMAQ = ("sync", "scalar", "gpsimd")

class Sched:
    def __init__(self):
        self.entries = []
        self.cnt = {e: 0 for e in ENGINES}
        self.dcnt = {e: 0 for e in DMAQ}
        self.flushed = {e: 0 for e in DMAQ}
        self.last_w = {}
        self.readers = {}
        self.flushed_vals = {e: {} for e in DMAQ}

    def _flush(self, q, k):
        if k <= self.flushed[q]:
            return self.flushed_vals[q][k]
        tot = self.dcnt[q]
        self.entries.append((q, None, [(q, "d", 16 * tot, "flush")], 0))
        for i in range(self.flushed[q] + 1, tot + 1):
            self.flushed_vals[q][i] = 16 * tot
        self.flushed[q] = tot
        return 16 * tot

    def op(self, eng, emit, reads=(), writes=(), inc=1):
        is_dma = inc >= 16
        deps = []
        for b in reads:
            w = self.last_w.get(b)
            if w is not None:
                deps.append(w + ("raw",))
            if b.startswith("ps"):
                for r in self.readers.get(b, ()):
                    if r[0] != eng:
                        deps.append(r + ("rr",))
        for b in writes:
            w = self.last_w.get(b)
            if w is not None:
                deps.append(w + ("waw",))
            deps.extend(r + ("war",) for r in self.readers.get(b, ()))
        waits = []
        for (weng, wstream, wcnt, kind) in deps:
            if wstream == "d":
                val = self._flush(weng, wcnt)
                waits.append((weng, "d", val, kind))
            else:
                waits.append((weng, "c", wcnt, kind))
        if is_dma:
            self.dcnt[eng] += 1
            done = (eng, "d", self.dcnt[eng])
        else:
            self.cnt[eng] += 1
            done = (eng, "c", self.cnt[eng])
        for b in reads:
            self.readers.setdefault(b, []).append(done)
        for b in writes:
            self.last_w[b] = done
            self.readers[b] = []
        self.entries.append((eng, emit, waits, inc))
        return done

    def barrier(self):
        waitsets = {}
        for e in ENGINES:
            snap = [(o, "c", self.cnt[o], "bar") for o in ENGINES if o != e and self.cnt[o]]
            for q in DMAQ:
                if self.dcnt[q]:
                    val = self._flush(q, self.dcnt[q])
                    if q != e:
                        snap.append((q, "d", val, "bar"))
            waitsets[e] = snap
        for e in ENGINES:
            self.entries.append((e, None, waitsets[e], 0))

    def replay(self, eng_name, handle, sems, max_entries=None):
        waited = {}
        ents = self.entries if max_entries is None else self.entries[:max_entries]
        for eng, emit, waits, inc in ents:
            if eng != eng_name:
                continue
            for (weng, wstream, wval, kind) in waits:
                if weng == eng_name and wstream == "c" and kind == "waw":
                    continue
                key = (weng, wstream)
                if wval > waited.get(key, 0):
                    handle.wait_ge(sems[key], wval)
                    waited[key] = wval
            if emit is not None:
                ins = emit(handle)
                ins.then_inc(sems[(eng_name, "d" if inc >= 16 else "c")], inc)


def ceil_div(a, b):
    return (a + b - 1) // b


def build_core_program(debug_taps=()):
    nc = bass.Bass("TRN2", target_bir_lowering=False)
    sch = Sched()

    def dram_in(name, shape, dtype=F32):
        return nc.dram_tensor(name, shape, dtype, kind="ExternalInput").ap()

    def scratch(name, shape, dtype):
        kind = "ExternalOutput" if name in debug_taps else "Internal"
        return nc.dram_tensor(name, shape, dtype, kind=kind).ap()

    x_in = dram_in("x", [L, D])
    convw = dram_in("convw", [128, 32])
    wq_in = dram_in("wq", [D, DK])
    wk_in = dram_in("wk", [D, DK])
    wv_in = dram_in("wv", [D, DV])
    wbt_in = dram_in("wbt", [128, 32])
    wg_in = dram_in("wg", [D, DV])
    wo_in = dram_in("wo", [DV, D])
    wgate_in = dram_in("wgate", [I_FFN, D])
    wup_in = dram_in("wup", [I_FFN, D])
    wdown_in = dram_in("wdown", [D, I_FFN])
    nwrep4 = dram_in("nwrep4", [128, 1024])
    identb = dram_in("identb", [128, 128], BF16)
    onesb = dram_in("onesb", [128, 128], BF16)
    maskag = dram_in("maskag", [128, 1024], BF16)
    maskaq = dram_in("maskaq", [128, 512], BF16)
    selh_in = dram_in("selh", [4, 512], BF16)

    out_t = nc.dram_tensor("out", [L, D], F32, kind="ExternalOutput").ap()

    xnfm_s = scratch("xnfm_s", [D, L], BF16)
    xq_s = scratch("xq_s", [L, D], BF16)
    wgqT_s = scratch("wgqT_s", [D, I_FFN], BF16)
    wuqT_s = scratch("wuqT_s", [D, I_FFN], BF16)
    wdqT_s = scratch("wdqT_s", [I_FFN, D], BF16)
    ap_s = scratch("ap_s", [L, DV], BF16)

    def psum(name, shape, dtype):
        t = nc.psum_tensor(name, shape, dtype)
        h = t.__enter__()
        return h

    psS = psum("psS", [128, 1024], F32)
    psF = psum("psF", [128, 1024], F32)
    psV = psum("psV", [128, 1024], F32)
    psT = psum("psT", [128, 1024], BF16)
    ps7 = psum("ps7", [128, 512], F32)

    sb_stack = []

    def sbuf(name, shape, dtype):
        t = nc.sbuf_tensor(name, shape, dtype)
        h = t.__enter__()
        sb_stack.append(t)
        return h

    cw_sb = sbuf("cw_sb", [128, 32], F32)
    nw_sb = sbuf("nw_sb", [128, 1024], BF16)
    idn_sb = sbuf("idn_sb", [128, 128], BF16)
    ones_sb = sbuf("ones_sb", [128, 128], BF16)
    mag_sb = sbuf("mag_sb", [128, 1024], BF16)
    maq_sb = sbuf("maq_sb", [128, 512], BF16)
    qc3_sb = sbuf("qc3_sb", [128, 1], F32)
    selh_sb = sbuf("selh_sb", [4, 512], BF16)

    def S(e):
        return sch.op(*e[0:2], **e[2]) if False else None

    sch.op("sync", lambda s: s.dma_start(out=cw_sb[:], in_=convw[:]), writes=["cw_sb"], inc=16)
    sch.op("gpsimd", lambda g: g.dma_start(out=nw_sb[:], in_=nwrep4[:]), writes=["nw_sb"], inc=16)
    sch.op("sync", lambda s: s.dma_start(out=idn_sb[:], in_=identb[:]), writes=["idn_sb"], inc=16)
    sch.op("sync", lambda s: s.dma_start(out=ones_sb[:], in_=onesb[:]), writes=["ones_sb"], inc=16)
    sch.op("sync", lambda s: s.dma_start(out=mag_sb[:], in_=maskag[:]), writes=["mag_sb"], inc=16)
    sch.op("sync", lambda s: s.dma_start(out=maq_sb[:], in_=maskaq[:]), writes=["maq_sb"], inc=16)
    sch.op("vector", lambda v: v.memset(qc3_sb[:], 3.0625), writes=["qc3"])
    sch.op("sync", lambda s: s.dma_start(out=selh_sb[:], in_=selh_in[:]), writes=["selh"], inc=16)

    def quant_stage1(src_ap, src_key, n, bufs):
        kp = bufs["kp"]
        nb = n // 16
        am_ap, rin_ap, sc_ap = bufs["am"][:, 0:nb], bufs["rinv"][:, 0:nb], bufs["sch"][:, 0:nb]
        src3 = src_ap.rearrange("p (b c) -> p b c", c=16)
        sch.op("vector",
               lambda v: v.reduce_max(am_ap, src3, axis=AX.X, apply_absolute_value=True),
               reads=[src_key], writes=[kp + "am"])
        sch.op("vector",
               lambda v: v.tensor_scalar(sc_ap, am_ap, 1.0 / 6.0, 1e-12,
                                         op0=AOP.mult, op1=AOP.max),
               reads=[kp + "am"], writes=[kp + "sc"])
        sch.op("vector",
               lambda v: v.reciprocal(rin_ap, sc_ap),
               reads=[kp + "sc"], writes=[kp + "rinv"])
        sch.op("vector",
               lambda v: v.tensor_scalar_mul(sc_ap, sc_ap, 0.5),
               reads=[kp + "sc"], writes=[kp + "sc"])

    def quant_stage2(src_ap, src_key, n, bufs):
        kp = bufs["kp"]
        nb = n // 16
        rin_ap = bufs["rinv"][:, 0:nb]
        src3 = src_ap.rearrange("p (b c) -> p b c", c=16)
        rinb = rin_ap.unsqueeze(2).broadcast_to([128, nb, 16])
        t3 = bufs["t"][:, 0:n].rearrange("p (b c) -> p b c", c=16)
        sch.op("gpsimd",
               lambda g: g.tensor_tensor(t3, src3, rinb, op=AOP.mult),
               reads=[src_key, kp + "rinv"], writes=[kp + "t"])

    def quant_stage3(n, bufs):
        kp = bufs["kp"]
        t_ap = bufs["t"][:, 0:n]
        s_ap, h_ap, sg_ap = bufs["s"][:, 0:n], bufs["h"][:, 0:n], bufs["sg"][:, 0:n]
        sch.op("vector",
               lambda v: v._custom_dve(QLO, out=s_ap, in0=t_ap, in1=qc3_sb[:],
                                       s0=0.0625, s1=0.5625, imm2=1.5625),
               reads=[kp + "t", "qc3"], writes=[kp + "s"])
        sch.op("vector",
               lambda v: v._custom_dve(QHI, out=h_ap, in0=t_ap, s0=6.25, s1=12.25, imm2=25.0),
               reads=[kp + "t"], writes=[kp + "h"])
        sch.op("scalar",
               lambda sc: sc.activation(sg_ap, t_ap, ACTF.Sign),
               reads=[kp + "t"], writes=[kp + "sg"])

    def quant_stage4(n, bufs):
        kp = bufs["kp"]
        nb = n // 16
        t_ap = bufs["t"][:, 0:n]
        s_ap, h_ap = bufs["s"][:, 0:n], bufs["h"][:, 0:n]
        sc_ap = bufs["sch"][:, 0:nb]
        sch.op("vector",
               lambda v: v.tensor_add(s_ap, h_ap, s_ap),
               reads=[kp + "h", kp + "s"], writes=[kp + "s"])
        scb = sc_ap.unsqueeze(2).broadcast_to([128, nb, 16])
        s3 = s_ap.rearrange("p (b c) -> p b c", c=16)
        t3 = t_ap.rearrange("p (b c) -> p b c", c=16)
        sch.op("gpsimd",
               lambda g: g.tensor_tensor(t3, s3, scb, op=AOP.mult),
               reads=[kp + "s", kp + "sc"], writes=[kp + "t"])

    def quant_stage5(dst_ap, dst_key, n, bufs, eng):
        kp = bufs["kp"]
        t_ap, sg_ap = bufs["t"][:, 0:n], bufs["sg"][:, 0:n]
        sch.op(eng,
               lambda e: e.tensor_tensor(dst_ap, t_ap, sg_ap, op=AOP.mult),
               reads=[kp + "t", kp + "sg"], writes=[dst_key])

    def emit_quant(tag, src_ap, src_key, dst_ap, dst_key, n, bufs):
        kp = bufs["kp"]
        nb = n // 16
        t_ap, s_ap, h_ap = bufs["t"][:, 0:n], bufs["s"][:, 0:n], bufs["h"][:, 0:n]
        sg_ap = bufs["sg"][:, 0:n]
        am_ap, rin_ap, sc_ap = bufs["am"][:, 0:nb], bufs["rinv"][:, 0:nb], bufs["sch"][:, 0:nb]
        src3 = src_ap.rearrange("p (b c) -> p b c", c=16)
        sch.op("vector",
               lambda v: v.reduce_max(am_ap, src3, axis=AX.X, apply_absolute_value=True),
               reads=[src_key], writes=[kp + "am"])
        sch.op("vector",
               lambda v: v.tensor_scalar(sc_ap, am_ap, 1.0 / 6.0, 1e-12,
                                         op0=AOP.mult, op1=AOP.max),
               reads=[kp + "am"], writes=[kp + "sc"])
        sch.op("vector",
               lambda v: v.reciprocal(rin_ap, sc_ap),
               reads=[kp + "sc"], writes=[kp + "rinv"])
        sch.op("vector",
               lambda v: v.tensor_scalar_mul(sc_ap, sc_ap, 0.5),
               reads=[kp + "sc"], writes=[kp + "sc"])
        rinb = rin_ap.unsqueeze(2).broadcast_to([128, nb, 16])
        t3 = t_ap.rearrange("p (b c) -> p b c", c=16)
        sch.op("gpsimd",
               lambda g: g.tensor_tensor(t3, src3, rinb, op=AOP.mult),
               reads=[src_key, kp + "rinv"], writes=[kp + "t"])
        sch.op("vector",
               lambda v: v._custom_dve(QLO, out=s_ap, in0=t_ap, in1=qc3_sb[:],
                                       s0=0.0625, s1=0.5625, imm2=1.5625),
               reads=[kp + "t", "qc3"], writes=[kp + "s"])
        sch.op("vector",
               lambda v: v._custom_dve(QHI, out=h_ap, in0=t_ap, s0=6.25, s1=12.25, imm2=25.0),
               reads=[kp + "t"], writes=[kp + "h"])
        sch.op("scalar",
               lambda sc: sc.activation(sg_ap, t_ap, ACTF.Sign),
               reads=[kp + "t"], writes=[kp + "sg"])
        sch.op("vector",
               lambda v: v.tensor_add(s_ap, h_ap, s_ap),
               reads=[kp + "h", kp + "s"], writes=[kp + "s"])
        scb = sc_ap.unsqueeze(2).broadcast_to([128, nb, 16])
        s3 = s_ap.rearrange("p (b c) -> p b c", c=16)
        sch.op("gpsimd",
               lambda g: g.tensor_tensor(t3, s3, scb, op=AOP.mult),
               reads=[kp + "s", kp + "sc"], writes=[kp + "t"])
        sch.op("gpsimd",
               lambda g: g.tensor_tensor(dst_ap, t_ap, sg_ap, op=AOP.mult),
               reads=[kp + "t", kp + "sg"], writes=[dst_key])

    xt = [sbuf(f"xt{i}", [128, D], F32) for i in range(2)]
    xnt = [sbuf("xnt0", [128, D], F32)]
    xnbt = [sbuf(f"xnbt{i}", [128, D], BF16) for i in range(2)]
    xqt = [sbuf("xqt0", [128, D], BF16)]
    stat_sx = sbuf("stat_sx", [128, NT], F32)
    stat_sq = sbuf("stat_sq", [128, NT], F32)
    stat_m = sbuf("stat_m", [128, NT], F32)
    stat_iv = sbuf("stat_iv", [128, NT], F32)
    qb_t = sbuf("qb_t", [128, D], F32)
    qb_h = sbuf("qb_h", [128, D], F32)
    qb_sg = sbuf("qb_sg", [128, D], BF16)
    qb_qs = sbuf("qb_qs", [128, 1024], BF16)
    qb_qh = sbuf("qb_qh", [128, 1024], BF16)
    qb_am = sbuf("qb_am", [128, 64], F32)
    qb_ri = sbuf("qb_ri", [128, 64], F32)
    qb_sc = sbuf("qb_sc", [128, 64], F32)
    qset0 = {"t": qb_t[:], "s": qb_qs[:], "h": qb_qh[:], "sg": qb_sg[:],
             "am": qb_am[:], "rinv": qb_ri[:], "sch": qb_sc[:], "kp": "qb."}



    aw_f32_g = nc.sbuf_tensor("aw_f32", [128, 14080], F32)
    aw_f32 = aw_f32_g.__enter__()
    aw_bf_g = nc.sbuf_tensor("aw_bf", [128, 14336], BF16)
    aw_bf = aw_bf_g.__enter__()
    aw_ac_g = nc.sbuf_tensor("aw_ac", [128, 22528], BF16)
    aw_ac = aw_ac_g.__enter__()

    def _aw_qset(s):
        b0 = 3072 * s
        f0 = 11264 + 192 * s
        return {"t": aw_f32[:, 7168 + 1024 * s:8192 + 1024 * s],
                "s": aw_bf[:, b0:b0 + 1024], "h": aw_bf[:, b0 + 1024:b0 + 2048],
                "sg": aw_bf[:, b0 + 2048:b0 + 3072],
                "am": aw_f32[:, f0:f0 + 64], "rinv": aw_f32[:, f0 + 64:f0 + 128],
                "sch": aw_f32[:, f0 + 128:f0 + 192], "kp": f"a{s + 1}."}
    QSETS_AW = [qset0] + [_aw_qset(s) for s in range(4)]
    qset_c0 = {"t": qb_t[:, 0:512], "s": qb_qs[:, 0:512], "h": qb_qh[:, 0:512],
               "sg": qb_sg[:, 0:512], "am": qb_am[:, 0:32], "rinv": qb_ri[:, 0:32],
               "sch": qb_sc[:, 0:32], "kp": "qb."}
    qset_c1 = {"t": qb_t[:, 512:1024], "s": qb_qs[:, 512:1024], "h": qb_qh[:, 512:1024],
               "sg": qb_sg[:, 512:1024], "am": qb_am[:, 32:64], "rinv": qb_ri[:, 32:64],
               "sch": qb_sc[:, 32:64], "kp": "qc."}
    QSETS_C = [qset_c0, qset_c1]

    ld_pool = [(xt[0][:], "xt0"), (xt[1][:], "xt1")] + [
        (aw_f32[:, 1024 * j:1024 * (j + 1)], f"awl{j}") for j in range(4)]
    xn_pool = [(xnt[0][:], "xnt0")] + [
        (aw_f32[:, 4096 + 1024 * j:5120 + 1024 * j], f"awx{j}") for j in range(3)]
    out_pool = [(xqt[0][:], "xqt0"), (aw_bf[:, 12288:13312], "awo1"),
                (aw_bf[:, 13312:14336], "awo2")]
    sq_pool = [(aw_f32[:, 12032:13056], "awsq0"), (aw_f32[:, 13056:14080], "awsq1")]

    wjobs = []
    for (wsrc, wdstT, rows, cols, wtag) in ((wgate_in, wgqT_s, I_FFN, D, "wgT"),
                                            (wup_in, wuqT_s, I_FFN, D, "wuT"),
                                            (wdown_in, wdqT_s, D, I_FFN, "wdT")):
        for wi in range(rows // 128):
            for c0 in range(0, cols, 1024):
                wjobs.append((wsrc, wdstT, wi, c0, min(1024, cols - c0), wtag))
    tiles = [("A", i) for i in range(NT)] + [("W", w) for w in wjobs]
    WLAST = {}
    for gg, (kind_, info_) in enumerate(tiles):
        if kind_ == "W":
            WLAST[info_[5]] = gg

    def emit_stage(st, g):
        kind, info = tiles[g]
        la, kl = ld_pool[g % 6]
        qs = QSETS_AW[g % 5]
        oa, ko = out_pool[g % 3]
        oeng = "gpsimd" if g % 2 else "vector"
        if kind == "A":
            i = info
            xna, kxn = xn_pool[g % 4]
            xnba, kxnb = xnbt[g % 2][:], f"xnbt{g % 2}"
            sqa, ksq = sq_pool[g % 2]
            stp = f"st{i}."
            if st == 0:
                sch.op("scalar", lambda s: s.dma_start(out=la[:], in_=x_in[i * 128:(i + 1) * 128, :]),
                       writes=[kl], inc=16)
            elif st == 1:
                sch.op("vector", lambda v: v.reduce_sum(stat_sx[:, i:i + 1], la[:], axis=AX.X),
                       reads=[kl], writes=[stp + "sx"])
                sch.op("scalar", lambda sc: sc.activation(sqa[:], la[:], ACTF.Square,
                                                          accum_out=stat_sq[:, i:i + 1]),
                       reads=[kl], writes=[stp + "sq", ksq])
            elif st == 2:
                sch.op("vector", lambda v: v.tensor_scalar_mul(stat_m[:, i:i + 1], stat_sx[:, i:i + 1], 1.0 / 1024.0),
                       reads=[stp + "sx"], writes=[stp + "m"])
                sch.op("vector", lambda v: v.tensor_scalar_mul(stat_iv[:, i:i + 1], stat_sq[:, i:i + 1], 1.0 / 1024.0),
                       reads=[stp + "sq"], writes=[stp + "iv"])
                sch.op("vector", lambda v: v.tensor_mul(stat_sq[:, i:i + 1],
                                                        stat_m[:, i:i + 1], stat_m[:, i:i + 1]),
                       reads=[stp + "m", stp + "sq"], writes=[stp + "sq"])
                sch.op("vector", lambda v: v.tensor_sub(stat_iv[:, i:i + 1], stat_iv[:, i:i + 1],
                                                        stat_sq[:, i:i + 1]),
                       reads=[stp + "iv", stp + "sq"], writes=[stp + "iv"])
                sch.op("vector", lambda v: v.tensor_scalar_add(stat_iv[:, i:i + 1], stat_iv[:, i:i + 1],
                                                               EPS_LN),
                       reads=[stp + "iv"], writes=[stp + "iv"])
                sch.op("scalar", lambda sc: sc.activation(stat_iv[:, i:i + 1], stat_iv[:, i:i + 1],
                                                          ACTF.Sqrt),
                       reads=[stp + "iv"], writes=[stp + "iv"])
                sch.op("vector", lambda v: v.reciprocal(stat_iv[:, i:i + 1], stat_iv[:, i:i + 1]),
                       reads=[stp + "iv"], writes=[stp + "iv"])
                sch.op("vector", lambda v: v.tensor_scalar(
                           xna[:], la[:], stat_m[:, i:i + 1], stat_iv[:, i:i + 1],
                           op0=AOP.subtract, op1=AOP.mult),
                       reads=[kl, stp + "m", stp + "iv"], writes=[kxn])
            elif st == 3:
                quant_stage1(xna, kxn, D, qs)
                sch.op("scalar", lambda sc: sc.activation(xnba, xna, ACTF.Copy),
                       reads=[kxn], writes=[kxnb])
            elif st == 4:
                quant_stage2(xna, kxn, D, qs)
                for j in range(8):
                    sch.op("tensor", lambda t, j=j: t.transpose(
                               psT[:, j * 128:(j + 1) * 128], xnba[:, j * 128:(j + 1) * 128],
                               idn_sb[:]),
                           reads=[kxnb, "idn_sb"], writes=["psT"])
                for j in range(8):
                    sch.op("scalar", lambda sc, j=j: sc.activation(
                               aw_ac[:, j * 2048 + i * 128:j * 2048 + (i + 1) * 128],
                               psT[:, j * 128:(j + 1) * 128], ACTF.Copy),
                           reads=["psT"], writes=["acc"])
                if i == NT - 1:
                    for kc in range(8):
                        sch.op("sync", lambda s, kc=kc: s.dma_start(
                                   out=xnfm_s[kc * 128:(kc + 1) * 128, :],
                                   in_=aw_ac[:, kc * 2048:(kc + 1) * 2048]),
                               reads=["acc"], writes=["xnfm"], inc=16)
            elif st == 5:
                quant_stage3(D, qs)
            elif st == 6:
                quant_stage4(D, qs)
            elif st == 7:
                quant_stage5(oa[:, 0:D], ko, D, qs, oeng)
                sch.op("sync", lambda s: s.dma_start(out=xq_s[i * 128:(i + 1) * 128, :], in_=oa[:, 0:D]),
                       reads=[ko], writes=["xq_s"], inc=16)
        else:
            wsrc, wdstT, wi, c0, cw, wtag = info
            if st == 0:
                sch.op("scalar", lambda s: s.dma_start(out=la[:, 0:cw],
                                                       in_=wsrc[wi * 128:(wi + 1) * 128, c0:c0 + cw]),
                       writes=[kl], inc=16)
            elif st == 3:
                quant_stage1(la[:, 0:cw], kl, cw, qs)
            elif st == 4:
                quant_stage2(la[:, 0:cw], kl, cw, qs)
            elif st == 5:
                quant_stage3(cw, qs)
            elif st == 6:
                quant_stage4(cw, qs)
            elif st == 7:
                quant_stage5(oa[:, 0:cw], ko, cw, qs, oeng)
            elif st == 8:
                for j in range(cw // 128):
                    sch.op("tensor", lambda t, j=j: t.transpose(
                               psT[:, j * 128:(j + 1) * 128], oa[:, j * 128:(j + 1) * 128],
                               idn_sb[:]),
                           reads=[ko, "idn_sb"], writes=["psT"])
                for j in range(cw // 128):
                    if wtag == "wdT":
                        off = (c0 // 128 + j) * 1024 + wi * 128
                    else:
                        off = j * 2816 + wi * 128
                    sch.op("scalar", lambda s, j=j, off=off: s.activation(
                               aw_ac[:, off:off + 128], psT[:, j * 128:(j + 1) * 128], ACTF.Copy),
                           reads=["psT"], writes=["acc"])
                if g == WLAST[wtag]:
                    ncols = 1024 if wtag == "wdT" else 2816
                    for ch in range((I_FFN if wtag == "wdT" else D) // 128):
                        sch.op("sync", lambda s, ch=ch, ncols=ncols: s.dma_start(
                                   out=wdstT[ch * 128:(ch + 1) * 128, :],
                                   in_=aw_ac[:, ch * ncols:(ch + 1) * ncols]),
                               reads=["acc"], writes=[wtag], inc=16)

    NST = 10
    for step in range(len(tiles) + NST - 1):
        for st in range(NST - 1, -1, -1):
            g = step - st
            if 0 <= g < len(tiles):
                emit_stage(st, g)

    sch.barrier()
    sch.marks = {"AW": len(sch.entries)}
    aw_ac_g.__exit__(None, None, None)
    aw_bf_g.__exit__(None, None, None)
    aw_f32_g.__exit__(None, None, None)

    h_sb = sbuf("h_sb", [128, 8 * 2048], BF16)
    qkk = sbuf("qkk", [128, 3 * 8192], BF16)
    wvgo = sbuf("wvgo", [128, 2 * 8192], BF16)
    wqk = sbuf("wqk", [128, 8192], BF16)
    dp = sbuf("dp", [128, 8192], BF16)
    dp2 = sbuf("dp2", [128, 4096], BF16)
    hpre = sbuf("hpre", [128, 2048], F32)
    betab = sbuf("betab", [4, 2048], BF16)
    btm = sbuf("btm", [128, 64], F32)
    btmn = sbuf("btmn", [128, 64], F32)
    wbt_sb = sbuf("wbt_sb", [128, 32], BF16)
    Sbf = sbuf("Sbf", [128, 1024], BF16)

    QN, KN, KB = 0, 8192, 16384
    AG0 = 0
    PW = {2: 1024, 4: 2048, 8: 3072, 16: 4096, 32: 5120}
    G64S, B1S = 6144, 6656
    B2S, B3S = 7168, 7680
    AQT, KCT = 0, 512
    U0, UA, UB = 1024, 2048, 3072
    vc_ap, vc_key = xnbt[0][:], "xnbt0"
    u0n_ap, u0n_key = xnbt[1][:], "xnbt1"
    uc_ap, uc_key = xqt[0][:], "xqt0"
    apre_ap, apre_key = qb_sg[:], "qb.sg"



    for kc in range(8):
        sch.op("gpsimd", lambda g, kc=kc: g.dma_start(out=wqk[:, kc * 512:(kc + 1) * 512],
                                                      in_=wq_in[kc * 128:(kc + 1) * 128, :]),
               writes=["wqk"], inc=16)
        sch.op("gpsimd", lambda g, kc=kc: g.dma_start(out=wqk[:, 4096 + kc * 512:4096 + (kc + 1) * 512],
                                                      in_=wk_in[kc * 128:(kc + 1) * 128, :]),
               writes=["wqk"], inc=16)
        sch.op("gpsimd", lambda g, kc=kc: g.dma_start(out=wvgo[:, kc * 1024:(kc + 1) * 1024],
                                                      in_=wv_in[kc * 128:(kc + 1) * 128, :]),
               writes=["wv"], inc=16)
        sch.op("gpsimd", lambda g, kc=kc: g.dma_start(out=wvgo[:, 8192 + kc * 1024:8192 + (kc + 1) * 1024],
                                                      in_=wg_in[kc * 128:(kc + 1) * 128, :]),
               writes=["wg"], inc=16)
    sch.op("gpsimd", lambda g: g.dma_start(out=wbt_sb[:], in_=wbt_in[:]), writes=["wbt"], inc=16)

    xnfm = [(qkk[:, 0:2048], "qkkA"), (qkk[:, 2048:4096], "qkkB")]
    for kc in range(8):
        xa, kx = xnfm[kc % 2]
        sch.op("sync", lambda s, kc=kc, xa=xa: s.dma_start(
                   out=xa, in_=xnfm_s[kc * 128:(kc + 1) * 128, :]),
               reads=["xnfm"], writes=[kx], inc=16)
        sch.op("vector", lambda v, kc=kc, xa=xa: v.tensor_scalar_mul(
                   hpre[:], xa, cw_sb[:, kc * 4 + 3:kc * 4 + 4]),
               reads=[kx, "cw_sb"], writes=["hpre"])
        for tap in (2, 1, 0):
            sh = 3 - tap
            sch.op("vector", lambda v, kc=kc, xa=xa, tap=tap, sh=sh: v.scalar_tensor_tensor(
                       hpre[:, sh:2048], xa[:, 0:2048 - sh], cw_sb[:, kc * 4 + tap:kc * 4 + tap + 1],
                       hpre[:, sh:2048], op0=AOP.mult, op1=AOP.add),
                   reads=[kx, "cw_sb", "hpre"], writes=["hpre"])
        sch.op("scalar", lambda s, kc=kc: s.activation(h_sb[:, kc * 2048:(kc + 1) * 2048], hpre[:],
                                                       ACTF.Silu),
               reads=["hpre"], writes=["h"])

    if "h_s" in debug_taps:
        h_s = scratch("h_s", [128, 8 * 2048], BF16)
        sch.op("sync", lambda sy: sy.dma_start(out=h_s[:], in_=h_sb[:]),
               reads=["h"], writes=["h_s"], inc=16)

    for g4 in range(4):
        for kc in range(8):
            sch.op("tensor", lambda t, g4=g4, kc=kc: t.matmul(
                       ps7[0:4, 0:512], wbt_sb[:, kc * 4:(kc + 1) * 4],
                       h_sb[:, kc * 2048 + g4 * 512: kc * 2048 + (g4 + 1) * 512],
                       start=(kc == 0), stop=(kc == 7)),
                   reads=["wbt", "h"], writes=["ps7"])
        sch.op("scalar", lambda s, g4=g4: s.activation(betab[0:4, g4 * 512:(g4 + 1) * 512],
                                                       ps7[0:4, 0:512], ACTF.Sigmoid),
               reads=["ps7"], writes=["betab"])
    for c in range(NCH):
        sch.op("tensor", lambda t, c=c: t.transpose(psT[:, c * 4:(c + 1) * 4],
                                                    betab[0:4, c * 128:(c + 1) * 128],
                                                    idn_sb[0:4, 0:4]),
               reads=["betab", "idn_sb"], writes=["psT"])
    sch.op("vector", lambda v: v.tensor_copy(btm[:], psT[:, 0:64]),
           reads=["psT"], writes=["btm"])
    sch.op("vector", lambda v: v.tensor_scalar_mul(btmn[:], btm[:], -1.0),
           reads=["btm"], writes=["btmn"])

    pgctr = [0]
    pend = [None]

    def l2tail(info):
        (dst, sga, sgk, sqa, sqk, ria, rik, psq, psqk, fk) = info
        sch.op("tensor", lambda t, sga=sga, psq=psq: t.matmul(psq, ones_sb[:], sga,
                                                              start=True, stop=True),
               reads=[sgk, "ones_sb"], writes=[psqk])
        sch.op("scalar", lambda s, psq=psq, sqa=sqa: s.activation(sqa, psq, ACTF.Sqrt),
               reads=[psqk], writes=[sqk])
        sch.op("vector", lambda v, sqa=sqa: v.tensor_scalar_max(sqa, sqa, 1e-12),
               reads=[sqk], writes=[sqk])
        sch.op("vector", lambda v, sqa=sqa, ria=ria: v.reciprocal(ria, sqa),
               reads=[sqk], writes=[rik])
        sch.op("vector", lambda v, dst=dst, ria=ria, fk=fk: v.tensor_mul(
                   qkk[:, dst:dst + 512], qkk[:, dst:dst + 512], ria),
               reads=[fk, rik], writes=[fk])

    for which, w0, dst0 in (("q", 0, QN), ("k", 4096, KN)):
        for hh in range(H):
            for g4 in range(4):
                par = pgctr[0] % 2
                fk = f"qkg{pgctr[0] % 4}"
                pgctr[0] += 1
                pso = par * 512
                psk = f"psF{par}"
                psq, psqk = (ps7[:, 0:512], "ps7") if par == 0 else (psS[:, 0:512], "psS")
                sga, sgk = qb_sg[:, pso:pso + 512], f"qb.sg{par}"
                sqa, sqk = qb_t[:, pso:pso + 512], f"qb.t{par}"
                ria, rik = qb_h[:, pso:pso + 512], f"qb.h{par}"
                for kc in range(8):
                    sch.op("tensor", lambda t, hh=hh, g4=g4, kc=kc, w0=w0, pso=pso: t.matmul(
                               psF[:, pso:pso + 512],
                               wqk[:, w0 + kc * 512 + hh * 128: w0 + kc * 512 + (hh + 1) * 128],
                               h_sb[:, kc * 2048 + g4 * 512: kc * 2048 + (g4 + 1) * 512],
                               start=(kc == 0), stop=(kc == 7)),
                           reads=["wqk", "h"], writes=[psk])
                dst = dst0 + hh * 2048 + g4 * 512
                sch.op("scalar", lambda s, dst=dst, pso=pso, fk=fk: s.activation(
                           qkk[:, dst:dst + 512], psF[:, pso:pso + 512], ACTF.Copy),
                       reads=[psk], writes=[fk, "qkkA", "qkkB", "qkkX"])
                sch.op("scalar", lambda s, pso=pso, sga=sga: s.activation(
                           sga, psF[:, pso:pso + 512], ACTF.Square),
                       reads=[psk], writes=[sgk])
                if pend[0] is not None:
                    l2tail(pend[0])
                pend[0] = (dst, sga, sgk, sqa, sqk, ria, rik, psq, psqk, fk)
    l2tail(pend[0])
    sch.op("vector", lambda v: v.memset(stat_m[:, 0:1], 0.0),
           reads=["psF0", "psF1", "qb.sg0", "qb.sg1", "qb.t0", "qb.t1", "qb.h0", "qb.h1",
                  "qkg0", "qkg1", "qkg2", "qkg3"],
           writes=["psF", "qb.sg", "qb.t", "qb.h", "qkk"])
    for hh in range(H):
        for g4 in range(4):
            sch.op("tensor", lambda t, hh=hh, g4=g4: t.matmul(
                       psF[:, 0:512], selh_sb[0:4, hh * 128:(hh + 1) * 128],
                       betab[0:4, g4 * 512:(g4 + 1) * 512],
                       start=True, stop=True),
                   reads=["betab", "selh"], writes=["psF"])
            dst = KB + hh * 2048 + g4 * 512
            src = KN + hh * 2048 + g4 * 512
            sch.op("vector", lambda v, dst=dst, src=src: v.tensor_mul(
                       qkk[:, dst:dst + 512], qkk[:, src:src + 512], psF[:, 0:512]),
                   reads=["qkk", "psF"], writes=["qkk", "qkkX"])

    sch.marks["B1"] = len(sch.entries)
    if "qkk_s" in debug_taps:
        qkk_s = scratch("qkk_s", [128, 3 * 8192], BF16)
        sch.op("sync", lambda sy: sy.dma_start(out=qkk_s[:], in_=qkk[:]),
               reads=["qkk"], writes=["qkk_s"], inc=16)
    if "beta_s" in debug_taps:
        beta_s = scratch("beta_s", [4, 2048], BF16)
        sch.op("sync", lambda sy: sy.dma_start(out=beta_s[:], in_=betab[:]),
               reads=["betab"], writes=["beta_s"], inc=16)

    sch.marks["B3"] = len(sch.entries)
    def mm(out_ap, lhs_ap, rhs_ap, reads, wkey, start, stop=True):
        sch.op("tensor", lambda t: t.matmul(out_ap, lhs_ap, rhs_ap, start=start, stop=stop,
                                            skip_group_check=True),
               reads=list(reads), writes=[wkey])

    def qs_(hh, c):
        return qkk[:, QN + hh * 2048 + c * 128: QN + hh * 2048 + (c + 1) * 128]

    def ks_(hh, c):
        return qkk[:, KN + hh * 2048 + c * 128: KN + hh * 2048 + (c + 1) * 128]

    def kbs_(hh, c):
        return qkk[:, KB + hh * 2048 + c * 128: KB + hh * 2048 + (c + 1) * 128]

    U0, UA = 2048, 3072

    def vcP(c):
        return (vc_ap, vc_key) if c % 2 == 0 else (wqk[:, 0:1024], "wqk")

    def agP(c):
        return (dp[:, AG0:AG0 + 1024], "dp.ag") if c % 2 == 0 else (wqk[:, 1024:2048], "wqk")

    def g64P(c):
        return (dp[:, G64S:G64S + 1024], "dp.g64b1") if c % 2 == 0 else (wqk[:, 2048:3072], "wqk")

    def b23P(c):
        return (dp[:, B2S:B2S + 1024], "dp.b23") if c % 2 == 0 else (wqk[:, 3072:4096], "wqk")

    def aqtP(c):
        return (0, "dp2.aqt") if c % 2 == 0 else (512, "dp2.aqt2")

    def kctP(c):
        return (1024, "dp2.kct") if c % 2 == 0 else (1536, "dp2.kct2")

    def nwgP(c):
        return (hpre[:, 0:1024], "hpre0") if c % 2 == 0 else (hpre[:, 1024:2048], "hpre1")

    def front_stages(c):
        vca, vck = vcP(c)
        aga, agk = agP(c)
        g64a, g64k = g64P(c)
        b23a, b23k = b23P(c)
        aqo, aqk = aqtP(c)
        kco, kck = kctP(c)
        nwa, nwk = nwgP(c)

        def f0():
            for half in range(2):
                for kc in range(8):
                    mm(psV[:, half * 512:(half + 1) * 512],
                       h_sb[:, kc * 2048 + c * 128: kc * 2048 + (c + 1) * 128],
                       wvgo[:, kc * 1024 + half * 512: kc * 1024 + (half + 1) * 512],
                       ["h", "wv"], "psV", start=(kc == 0), stop=(kc == 7))

        def f1():
            sch.op("scalar", lambda sc: sc.copy(vca, psV[:]), reads=["psV"], writes=[vck])

        def f2():
            for half in range(2):
                for kc in range(8):
                    mm(psV[:, half * 512:(half + 1) * 512],
                       h_sb[:, kc * 2048 + c * 128: kc * 2048 + (c + 1) * 128],
                       wvgo[:, 8192 + kc * 1024 + half * 512: 8192 + kc * 1024 + (half + 1) * 512],
                       ["h", "wg"], "psV", start=(kc == 0), stop=(kc == 7))

        def f3():
            sch.op("scalar", lambda s: s.activation(xnt[0][:], psV[:], ACTF.Silu),
                   reads=["psV"], writes=["xnt0"])
            sch.op("vector", lambda v: v.tensor_mul(nwa, xnt[0][:], nw_sb[:]),
                   reads=["xnt0", "nw_sb"], writes=[nwk])

        def f4():
            for hh in range(H):
                col = hh * 256
                mm(psV[:, col:col + 128], kbs_(hh, c), ks_(hh, c), ["qkk"], "psV",
                   start=(col % 512 == 0))
                mm(psV[:, col + 128:col + 256], ks_(hh, c), kbs_(hh, c), ["qkk"], "psV",
                   start=False)

        def f5():
            sch.op("vector", lambda v: v.tensor_mul(aga, psV[:], mag_sb[:]),
                   reads=["psV", "mag_sb"], writes=[agk])

        def f6():
            for hh in range(H):
                mm(ps7[:, hh * 128:(hh + 1) * 128], ks_(hh, c), qs_(hh, c), ["qkk"], "ps7",
                   start=(hh == 0))
            sch.op("vector", lambda v: v.tensor_mul(dp2[:, aqo:aqo + 512], ps7[:], maq_sb[:]),
                   reads=["ps7", "maq_sb"], writes=[aqk])
            for hh in range(H):
                sch.op("tensor", lambda t, hh=hh: t.transpose(
                           psT[:, hh * 128:(hh + 1) * 128], ks_(hh, c), idn_sb[:]),
                       reads=["qkk", "idn_sb"], writes=["psT"])
            sch.op("vector", lambda v: v.tensor_copy(dp2[:, kco:kco + 512],
                                                     psT[:, 0:512].bitcast(BF16)),
                   reads=["psT"], writes=[kck])

        def mk_lev(lev, prev_ap, prev_key, slot):
            def f():
                for hh in range(H):
                    col = hh * 256
                    mm(psV[:, col:col + 128], prev_ap[:, col + 128:col + 256],
                       prev_ap[:, col:col + 128], [prev_key], "psV",
                       start=(col % 512 == 0))
                    mm(psV[:, col + 128:col + 256], prev_ap[:, col:col + 128],
                       prev_ap[:, col + 128:col + 256], [prev_key], "psV", start=False)
                sch.op("scalar", lambda sc: sc.copy(dp[:, slot:slot + 1024], psV[:]),
                       reads=["psV"], writes=["dp.p%d" % lev])
            return f

        def f12():
            for hh in range(H):
                cG, c32, c2, c4 = hh * 128, PW[32] + hh * 256, PW[2] + hh * 256, PW[4] + hh * 256
                mm(psV[:, cG:cG + 128], dp[:, c32:c32 + 128], dp[:, c32 + 128:c32 + 256],
                   ["dp.p32"], "psV", start=(cG % 512 == 0))
                cB = 512 + hh * 128
                mm(psV[:, cB:cB + 128], idn_sb[:], idn_sb[:], ["idn_sb"], "psV",
                   start=(cB % 512 == 0))
                mm(psV[:, cB:cB + 128], idn_sb[:], dp[:, c2 + 128:c2 + 256], ["dp.p2", "idn_sb"],
                   "psV", start=False)
                mm(psV[:, cB:cB + 128], idn_sb[:], dp[:, c4 + 128:c4 + 256], ["dp.p4", "idn_sb"],
                   "psV", start=False)
                mm(psV[:, cB:cB + 128], dp[:, c2:c2 + 128], dp[:, c4 + 128:c4 + 256],
                   ["dp.p2", "dp.p4"], "psV", start=False)
            sch.op("scalar", lambda sc: sc.copy(g64a, psV[:]), reads=["psV"], writes=[g64k])

        def f13():
            for hh in range(H):
                c8, c16, c32 = PW[8] + hh * 256, PW[16] + hh * 256, PW[32] + hh * 256
                cB2, cB3 = hh * 128, 512 + hh * 128
                g64c = hh * 128
                mm(psV[:, cB2:cB2 + 128], idn_sb[:], idn_sb[:], ["idn_sb"], "psV",
                   start=(cB2 % 512 == 0))
                mm(psV[:, cB2:cB2 + 128], idn_sb[:], dp[:, c8 + 128:c8 + 256], ["dp.p8"], "psV", start=False)
                mm(psV[:, cB2:cB2 + 128], idn_sb[:], dp[:, c16 + 128:c16 + 256], ["dp.p16"], "psV", start=False)
                mm(psV[:, cB2:cB2 + 128], dp[:, c8:c8 + 128], dp[:, c16 + 128:c16 + 256],
                   ["dp.p8", "dp.p16"], "psV", start=False)
                mm(psV[:, cB3:cB3 + 128], idn_sb[:], idn_sb[:], ["idn_sb"], "psV",
                   start=(cB3 % 512 == 0))
                mm(psV[:, cB3:cB3 + 128], idn_sb[:], dp[:, c32 + 128:c32 + 256], ["dp.p32"], "psV", start=False)
                mm(psV[:, cB3:cB3 + 128], idn_sb[:], g64a[:, g64c:g64c + 128], [g64k], "psV", start=False)
                mm(psV[:, cB3:cB3 + 128], dp[:, c32:c32 + 128], g64a[:, g64c:g64c + 128],
                   ["dp.p32", g64k], "psV", start=False)
            sch.op("scalar", lambda sc: sc.copy(b23a, psV[:]), reads=["psV"], writes=[b23k])

        return [f0, f1, f2, f3, f4, f5, f6,
                mk_lev(2, aga, agk, PW[2]),
                mk_lev(4, dp[:, PW[2]:PW[2] + 1024], "dp.p2", PW[4]),
                mk_lev(8, dp[:, PW[4]:PW[4] + 1024], "dp.p4", PW[8]),
                mk_lev(16, dp[:, PW[8]:PW[8] + 1024], "dp.p8", PW[16]),
                mk_lev(32, dp[:, PW[16]:PW[16] + 1024], "dp.p16", PW[32]),
                f12, f13]

    def flow_stages(c):
        vca, vck = vcP(c)
        aga, agk = agP(c)
        g64a, g64k = g64P(c)
        b23a, b23k = b23P(c)
        aqo, aqk = aqtP(c)
        kco, kck = kctP(c)
        nwa, nwk = nwgP(c)

        def l0():
            if c > 0:
                for hh in range(H):
                    col = hh * 256
                    mm(psF[:, col:col + 256], kbs_(hh, c), Sbf[:, hh * 256:(hh + 1) * 256],
                       ["qkk", "Sbf"], "psF", start=(col % 512 == 0))
                for hh in range(H):
                    col = hh * 256
                    sch.op("vector", lambda v, hh=hh, col=col: v.scalar_tensor_tensor(
                               dp2[:, U0 + col:U0 + col + 256], vca[:, col:col + 256],
                               btm[:, c * 4 + hh:c * 4 + hh + 1], psF[:, col:col + 256],
                               op0=AOP.mult, op1=AOP.subtract),
                           reads=[vck, "btm", "psF"], writes=["dp2.u0"])
                    sch.op("vector", lambda v, hh=hh, col=col: v.scalar_tensor_tensor(
                               u0n_ap[:, col:col + 256], vca[:, col:col + 256],
                               btmn[:, c * 4 + hh:c * 4 + hh + 1], psF[:, col:col + 256],
                               op0=AOP.mult, op1=AOP.add),
                           reads=[vck, "btmn", "psF"], writes=[u0n_key])
            else:
                for hh in range(H):
                    col = hh * 256
                    sch.op("vector", lambda v, col=col, hh=hh: v.tensor_scalar_mul(
                               dp2[:, U0 + col:U0 + col + 256], vca[:, col:col + 256],
                               btm[:, c * 4 + hh:c * 4 + hh + 1]),
                           reads=[vck, "btm"], writes=["dp2.u0"])
                    sch.op("vector", lambda v, col=col, hh=hh: v.tensor_scalar_mul(
                               u0n_ap[:, col:col + 256], vca[:, col:col + 256],
                               btmn[:, c * 4 + hh:c * 4 + hh + 1]),
                           reads=[vck, "btmn"], writes=[u0n_key])

        def l1():
            for hh in range(H):
                col = hh * 256
                mm(psF[:, col:col + 256], idn_sb[:], dp2[:, U0 + col:U0 + col + 256],
                   ["dp2.u0", "idn_sb"], "psF", start=(col % 512 == 0))
                mm(psF[:, col:col + 256], aga[:, col + 128:col + 256], u0n_ap[:, col:col + 256],
                   [agk, u0n_key], "psF", start=False)
            sch.op("scalar", lambda sc: sc.copy(dp2[:, UA:UA + 1024], psF[:]),
                   reads=["psF"], writes=["dp2.ua"])

        def mk_b(bmat_ap, bmat_key, boff, src_ap2, skey, dst_ap2, dkey):
            def f():
                for hh in range(H):
                    col = hh * 256
                    mm(psF[:, col:col + 256], bmat_ap[:, boff + hh * 128:boff + (hh + 1) * 128],
                       src_ap2[:, col:col + 256], [bmat_key, skey], "psF",
                       start=(col % 512 == 0))
                sch.op("scalar", lambda sc: sc.copy(dst_ap2, psF[:]),
                       reads=["psF"], writes=[dkey])
            return f

        def l5():
            for hh in range(H):
                col = hh * 256
                if c > 0:
                    mm(psF[:, col:col + 256], qs_(hh, c), Sbf[:, hh * 256:(hh + 1) * 256],
                       ["qkk", "Sbf"], "psF", start=(col % 512 == 0))
                    mm(psF[:, col:col + 256], dp2[:, aqo + hh * 128:aqo + (hh + 1) * 128],
                       dp2[:, U0 + col:U0 + col + 256], [aqk, "dp2.u0"], "psF", start=False)
                else:
                    mm(psF[:, col:col + 256], dp2[:, aqo + hh * 128:aqo + (hh + 1) * 128],
                       dp2[:, U0 + col:U0 + col + 256], [aqk, "dp2.u0"], "psF",
                       start=(col % 512 == 0))
            for hh in range(H):
                col = hh * 256
                mm(psS[:, col:col + 256], dp2[:, kco + hh * 128:kco + (hh + 1) * 128],
                   dp2[:, U0 + col:U0 + col + 256], [kck, "dp2.u0"], "psS",
                   start=(c == 0 and col % 512 == 0))
            sch.op("scalar", lambda sc: sc.copy(Sbf[:], psS[:]), reads=["psS"], writes=["Sbf"])

        def l6():
            sch.op("vector", lambda v: v.tensor_copy(qb_h[:], psF[:]),
                   reads=["psF"], writes=["qb.h"])
            for hh in range(H):
                col = hh * 256
                sch.op("vector", lambda v, hh=hh, col=col: v.tensor_tensor_reduce(
                           out=qb_qs[:, col:col + 256], in0=qb_h[:, col:col + 256],
                           in1=qb_h[:, col:col + 256], scale=1.0, scalar=0.0,
                           op0=AOP.mult, op1=AOP.add, accum_out=stat_sx[:, hh:hh + 1]),
                       reads=["qb.h"], writes=["stat_sx", "qb.s"])
            sch.op("vector", lambda v: v.tensor_scalar(stat_sx[:, 4:8], stat_sx[:, 0:4],
                                                       1.0 / 256.0, EPS_RMS, op0=AOP.mult, op1=AOP.add),
                   reads=["stat_sx"], writes=["stat_sx"])
            sch.op("scalar", lambda s: s.activation(stat_sx[:, 4:8], stat_sx[:, 4:8], ACTF.Sqrt),
                   reads=["stat_sx"], writes=["stat_sx"])
            sch.op("vector", lambda v: v.reciprocal(stat_sx[:, 4:8], stat_sx[:, 4:8]),
                   reads=["stat_sx"], writes=["stat_sx"])

        def l7():
            for hh in range(H):
                col = hh * 256
                sch.op("vector", lambda v, hh=hh, col=col: v.tensor_scalar_mul(
                           qb_t[:, col:col + 256], qb_h[:, col:col + 256], stat_sx[:, 4 + hh:5 + hh]),
                       reads=["qb.h", "stat_sx"], writes=["qb.t"])
            sch.op("vector", lambda v: v.tensor_mul(apre_ap, qb_t[:], nwa),
                   reads=["qb.t", nwk], writes=[apre_key])
            sch.op("sync", lambda s: s.dma_start(out=ap_s[c * 128:(c + 1) * 128, :],
                                                 in_=apre_ap),
                   reads=[apre_key], writes=["ap_s"], inc=16)

        return [l0, l1,
                mk_b(g64a, g64k, 512, dp2[:, UA:UA + 1024], "dp2.ua", qb_qh[:], "qb.qh"),
                mk_b(b23a, b23k, 0, qb_qh[:], "qb.qh", uc_ap, uc_key),
                mk_b(b23a, b23k, 512, uc_ap, uc_key, dp2[:, U0:U0 + 1024], "dp2.u0"),
                l5, l6, l7]

    _ORD = [(0, 0), (0, 1), (1, 0), (0, 2), (0, 3), (1, 1), (0, 4), (0, 5), (1, 2),
            (0, 6), (1, 3), (0, 7), (0, 8), (1, 4), (0, 9), (0, 10), (1, 5),
            (0, 11), (0, 12), (1, 6), (0, 13), (1, 7)]
    for f in front_stages(0):
        f()
    for c in range(NCH):
        fl = flow_stages(c)
        if c + 1 < NCH:
            fr = front_stages(c + 1)
            for which, idx in _ORD:
                (fr if which == 0 else fl)[idx]()
        else:
            for f in fl:
                f()

    sch.marks["B4"] = len(sch.entries)
    if "sfin_s" in debug_taps:
        sfin_s = scratch("sfin_s", [128, 1024], BF16)
        sch.op("sync", lambda sy: sy.dma_start(out=sfin_s[:], in_=Sbf[:]),
               reads=["Sbf"], writes=["sfin_s"], inc=16)

    NKC = I_FFN // 128
    wgT = [(qkk[:, f * 2816:(f + 1) * 2816], f"cg{f}") for f in range(8)]
    wuT = ([(h_sb[:, f * 2816:(f + 1) * 2816], f"cu{f}") for f in range(5)] +
           [(wvgo[:, (f - 5) * 2816:(f - 4) * 2816], f"cu{f}") for f in range(5, 8)])
    wdT = ([(wvgo[:, 9216 + j * 1024:9216 + (j + 1) * 1024], f"cd{j}") for j in range(7)] +
           [(wqk[:, (j - 7) * 1024:(j - 6) * 1024], f"cd{j}") for j in range(7, 15)] +
           [(dp[:, (j - 15) * 1024:(j - 14) * 1024], f"cd{j}") for j in range(15, 22)])
    CQ = ["sync", "scalar", "gpsimd"]
    cqi = [0]

    def cload(dst_ap, dkey, src_ap, skey, extra=(), q=None):
        if q is None:
            q = CQ[cqi[0] % 3]
            cqi[0] += 1
        sch.op(q, lambda s: s.dma_start(out=dst_ap, in_=src_ap),
               reads=[skey], writes=[dkey] + list(extra), inc=16)

    DPKEYS = ["dp.ag", "dp.p2", "dp.p4", "dp.p8", "dp.p16", "dp.p32", "dp.g64b1", "dp.b23"]
    for f in range(5):
        cload(wuT[f][0], wuT[f][1], wuqT_s[f * 128:(f + 1) * 128, :], "wuT",
              extra=["h"], q="gpsimd")
    for j in range(7):
        cload(wdT[j][0], wdT[j][1], wdqT_s[j * 128:(j + 1) * 128, :], "wdT",
              extra=["wg"], q="gpsimd")
    for j in range(15, 22):
        cload(wdT[j][0], wdT[j][1], wdqT_s[j * 128:(j + 1) * 128, :], "wdT",
              extra=DPKEYS, q="gpsimd")

    for kc in range(8):
        sch.op("gpsimd", lambda g, kc=kc: g.dma_start(out=wqk[:, kc * 1024:(kc + 1) * 1024],
                                                      in_=wo_in[kc * 128:(kc + 1) * 128, :]),
               writes=["wqk"], inc=16)
    apfm_b = [(wvgo[:, 0:4096], "wv"), (qkk[:, 2048:6144], "qkk")]

    def wo_prep(tb):
        apfm, apk = apfm_b[tb % 2]
        for sub in range(4):
            t = tb * 4 + sub
            lda = qkk[:, (sub % 2) * 1024:(sub % 2) * 1024 + 1024]
            sch.op("scalar", lambda s, t=t, lda=lda: s.dma_start(
                       out=lda, in_=ap_s[t * 128:(t + 1) * 128, :]),
                   reads=["ap_s", "qkk"], writes=["qkk"], inc=16)
            for f in range(8):
                sch.op("tensor", lambda tr, f=f, lda=lda: tr.transpose(
                           psT[:, f * 128:(f + 1) * 128], lda[:, f * 128:(f + 1) * 128],
                           idn_sb[:]),
                       reads=["qkk", "idn_sb"], writes=["psT"])
            for f in range(8):
                sch.op("scalar", lambda sc, f=f, sub=sub, apfm=apfm: sc.activation(
                           apfm[:, f * 512 + sub * 128:f * 512 + (sub + 1) * 128],
                           psT[:, f * 128:(f + 1) * 128], ACTF.Copy),
                       reads=["psT"], writes=[apk])

    wo_prep(0)
    for tb in range(4):
        apfm, apk = apfm_b[tb % 2]
        for sub in range(4):
            for g2 in range(2):
                for f in range(8):
                    mm(psF[:, g2 * 512:(g2 + 1) * 512],
                       apfm[:, f * 512 + sub * 128: f * 512 + (sub + 1) * 128],
                       wqk[:, f * 1024 + g2 * 512: f * 1024 + (g2 + 1) * 512],
                       [apk, "wqk"], "psF", start=(f == 0), stop=(f == 7))
            if sub == 0 and tb + 1 < 4:
                wo_prep(tb + 1)
            row = (tb * 4 + sub) * 128
            xa2, kx2 = (xt[0], "xt0") if sub % 2 == 0 else (xt[1], "xt1")
            ra2, kr2 = (xnt[0][:], "xnt0") if sub % 2 == 0 else (qb_t[:], "qb.t")
            sch.op("scalar", lambda s, row=row, xa2=xa2, kx2=kx2: s.dma_start(
                       out=xa2[:], in_=x_in[row:row + 128, :]),
                   reads=[kx2], writes=[kx2], inc=16)
            sch.op("vector", lambda v, xa2=xa2, ra2=ra2: v.tensor_add(ra2, xa2[:], psF[:]),
                   reads=[kx2, "psF"], writes=[kr2])
            sch.op("sync", lambda s, row=row, ra2=ra2, kr2=kr2: s.dma_start(
                       out=out_t[row:row + 128, :], in_=ra2),
                   reads=[kr2], writes=["out_partial"], inc=16)

    sch.barrier()

    sch.marks["B5"] = len(sch.entries)
    for f in range(8):
        cload(wgT[f][0], wgT[f][1], wgqT_s[f * 128:(f + 1) * 128, :], "wgT")
    for f in range(5, 8):
        cload(wuT[f][0], wuT[f][1], wuqT_s[f * 128:(f + 1) * 128, :], "wuT")
    for j in range(7, 15):
        cload(wdT[j][0], wdT[j][1], wdqT_s[j * 128:(j + 1) * 128, :], "wdT")

    xql = [(xnbt[0][:], "xnbt0"), (xnbt[1][:], "xnbt1")]
    xqTb = [(dp2[:, 0:1024], "cxq0"), (dp2[:, 1024:2048], "cxq1")]
    hqTb = [(dp2[:, 2048:2560], "chq0"), (dp2[:, 2560:3072], "chq1")]
    hqq = [(xqt[0][:, 0:512], "cq0"), (xqt[0][:, 512:1024], "cq1")]
    gsil_b = [(hpre[:, 0:512], "cgs0"), (hpre[:, 512:1024], "cgs1")]
    ht_b = [(hpre[:, 1024:1536], "cht0"), (hpre[:, 1536:2048], "cht1")]
    res_b = [(xt[0][:], "xt0", xnt[0][:], "xnt0"), (xt[1][:], "xt1", qb_h[:], "qb_h")]
    GRP = [(0, 512), (512, 512), (1024, 512), (1536, 512), (2048, 384), (2432, 384)]

    def c_xq_prep(tc):
        xa, kx = xql[tc % 2]
        qa, kq = xqTb[tc % 2]
        sch.op("scalar", lambda s: s.dma_start(out=xa, in_=xq_s[tc * 128:(tc + 1) * 128, :]),
               reads=["xq_s"], writes=[kx], inc=16)
        for j in range(8):
            sch.op("tensor", lambda t, j=j: t.transpose(psT[:, j * 128:(j + 1) * 128],
                                                        xa[:, j * 128:(j + 1) * 128], idn_sb[:]),
                   reads=[kx, "idn_sb"], writes=["psT"])
        sch.op("scalar", lambda s: s.activation(qa, psT[:], ACTF.Copy),
               reads=["psT"], writes=[kq])

    def c_gup(tc, g):
        gi0, gw = GRP[g]
        psu, psk = (psS, "psS") if g % 2 == 0 else (psV, "psV")
        qa, kq = xqTb[tc % 2]
        for f in range(8):
            mm(psu[:, 0:gw], qa[:, f * 128:(f + 1) * 128], wgT[f][0][:, gi0:gi0 + gw],
               [kq, wgT[f][1]], psk, start=(f == 0), stop=(f == 7))
        for f in range(8):
            mm(psu[:, 512:512 + gw], qa[:, f * 128:(f + 1) * 128], wuT[f][0][:, gi0:gi0 + gw],
               [kq, wuT[f][1]], psk, start=(f == 0), stop=(f == 7))

    def c_quant_a(tc, g):
        gi0, gw = GRP[g]
        par = g % 2
        psu, psk = (psS, "psS") if par == 0 else (psV, "psV")
        ga, kg = gsil_b[par]
        ha, kh = ht_b[par]
        sch.op("scalar", lambda s: s.activation(ga[:, 0:gw], psu[:, 0:gw], ACTF.Silu),
               reads=[psk], writes=[kg])
        sch.op("vector", lambda v: v.tensor_mul(ha[:, 0:gw], ga[:, 0:gw], psu[:, 512:512 + gw]),
               reads=[kg, psk], writes=[kh])
        quant_stage1(ha[:, 0:gw], kh, gw, QSETS_C[par])
        quant_stage2(ha[:, 0:gw], kh, gw, QSETS_C[par])

    def c_quant_b(tc, g):
        gi0, gw = GRP[g]
        par = g % 2
        qa2, kq2 = hqq[par]
        quant_stage3(gw, QSETS_C[par])
        quant_stage4(gw, QSETS_C[par])
        quant_stage5(qa2[:, 0:gw], kq2, gw, QSETS_C[par], "gpsimd")

    def c_tp(tc, g):
        gi0, gw = GRP[g]
        par = g % 2
        qa2, kq2 = hqq[par]
        ta, kt = hqTb[par]
        for j in range(gw // 128):
            sch.op("tensor", lambda t, j=j: t.transpose(psT[:, j * 128:(j + 1) * 128],
                                                        qa2[:, j * 128:(j + 1) * 128], idn_sb[:]),
                   reads=[kq2, "idn_sb"], writes=["psT"])
        sch.op("scalar", lambda s: s.activation(ta[:, 0:gw], psT[:, 0:gw], ACTF.Copy),
               reads=["psT"], writes=[kt])

    def c_dmm(tc, g):
        gi0, gw = GRP[g]
        ta, kt = hqTb[g % 2]
        for j in range(gw // 128):
            kc = gi0 // 128 + j
            wda, kwd = wdT[kc]
            mm(psF[:, 0:512], ta[:, j * 128:(j + 1) * 128], wda[:, 0:512],
               [kt, kwd], "psF", start=(kc == 0), stop=(kc == NKC - 1))
            mm(psF[:, 512:1024], ta[:, j * 128:(j + 1) * 128], wda[:, 512:1024],
               [kt, kwd], "psF", start=(kc == 0), stop=(kc == NKC - 1))

    def c_res(tc):
        xa, kx, ra, kr = res_b[tc % 2]
        sch.op("scalar", lambda s: s.dma_start(out=xa, in_=out_t[tc * 128:(tc + 1) * 128, :]),
               reads=["out_partial"], writes=[kx], inc=16)
        sch.op("vector", lambda v: v.tensor_add(ra, xa, psF[:]),
               reads=[kx, "psF"], writes=[kr])
        sch.op("sync", lambda s: s.dma_start(out=out_t[tc * 128:(tc + 1) * 128, :], in_=ra),
               reads=[kr], writes=["out_partial"], inc=16)

    NGR = NT * 6
    c_xq_prep(0)
    for m in range(NGR + 3):
        if 0 <= m - 2 < NGR:
            tcd, gd = divmod(m - 2, 6)
            c_tp(tcd, gd)
        if 0 <= m - 1 < NGR:
            tcq, gq = divmod(m - 1, 6)
            c_quant_b(tcq, gq)
        if m < NGR:
            tc, g = divmod(m, 6)
            if g == 4 and tc + 1 < NT:
                c_xq_prep(tc + 1)
            c_gup(tc, g)
            c_quant_a(tc, g)
        if 0 <= m - 3 < NGR:
            tcd, gd = divmod(m - 3, 6)
            c_dmm(tcd, gd)
            if gd == 5:
                c_res(tcd)

    sch.barrier()
    nc_sched = (nc, sch)
    return nc_sched, locals()


def finish_program(nc, sch, max_entries=None):
    sems = {}
    ctxs = []
    for e in ENGINES:
        c = nc.semaphore(f"sem_{e}")
        h = c.__enter__()
        ctxs.append(c)
        sems[(e, "c")] = h
    for e in DMAQ:
        c = nc.semaphore(f"semd_{e}")
        h = c.__enter__()
        ctxs.append(c)
        sems[(e, "d")] = h
    blk_ctx = nc.Block()
    blk = blk_ctx.__enter__()

    blk.sync(lambda h: sch.replay("sync", h, sems, max_entries))
    blk.scalar(lambda h: sch.replay("scalar", h, sems, max_entries))
    blk.vector(lambda h: sch.replay("vector", h, sems, max_entries))
    blk.tensor(lambda h: sch.replay("tensor", h, sems, max_entries))
    blk.gpsimd(lambda h: sch.replay("gpsimd", h, sems, max_entries))

    blk_ctx.__exit__(None, None, None)
    for c in reversed(ctxs):
        c.__exit__(None, None, None)
    mybir.codegen_inst_isa_subclasses(nc)
    return nc


def make_inputs_per_core(core, full):
    (x, convw, wq, wk, wv, wb, wg, nw, wo, wgate, wup, wdown) = full
    d = {
        "x": np.ascontiguousarray(x[core]),
        "convw": np.ascontiguousarray(convw.reshape(8, 128, 4).transpose(1, 0, 2).reshape(128, 32)),
        "wq": wq, "wk": wk, "wv": wv, "wg": wg, "wo": wo,
        "wbt": np.ascontiguousarray(wb.reshape(8, 128, 4).transpose(1, 0, 2).reshape(128, 32)),
        "wgate": wgate, "wup": wup, "wdown": wdown,
        "nwrep4": np.ascontiguousarray(np.tile(nw[None, :], (128, 4)).astype(np.float32)),
        "identb": np.eye(128, dtype=ml_dtypes.bfloat16),
        "onesb": np.ones((128, 128), dtype=ml_dtypes.bfloat16),
        "maskag": _mask_ag(),
        "maskaq": _mask_aq(),
        "selh": _selh(),
    }
    return d


def _mask_ag():
    sl = np.tril(np.ones((128, 128), np.float32), -1)
    su = np.triu(np.ones((128, 128), np.float32), 1)
    m = np.concatenate([sl, su] * 4, axis=1)[:, :1024]
    return np.ascontiguousarray(m.astype(ml_dtypes.bfloat16))


def _selh():
    m = np.zeros((4, 512), np.float32)
    for hh in range(4):
        m[hh, hh * 128:(hh + 1) * 128] = 1.0
    return np.ascontiguousarray(m.astype(ml_dtypes.bfloat16))


def _mask_aq():
    su = np.triu(np.ones((128, 128), np.float32), 0)
    return np.ascontiguousarray(np.concatenate([su] * 4, axis=1).astype(ml_dtypes.bfloat16))


def kernel(L0_attn_hidden_states, L0_attn_conv_w, L0_attn_Wq, L0_attn_Wk,
           L0_attn_Wv, L0_attn_Wb, L0_attn_Wg, L0_attn_norm_w, L0_attn_Wo,
           L0_ffn_w_gate, L0_ffn_w_up, L0_ffn_w_down):
    full = (np.asarray(L0_attn_hidden_states, np.float32),
            np.asarray(L0_attn_conv_w, np.float32),
            np.asarray(L0_attn_Wq, np.float32),
            np.asarray(L0_attn_Wk, np.float32),
            np.asarray(L0_attn_Wv, np.float32),
            np.asarray(L0_attn_Wb, np.float32),
            np.asarray(L0_attn_Wg, np.float32),
            np.asarray(L0_attn_norm_w, np.float32),
            np.asarray(L0_attn_Wo, np.float32),
            np.asarray(L0_ffn_w_gate, np.float32),
            np.asarray(L0_ffn_w_up, np.float32),
            np.asarray(L0_ffn_w_down, np.float32))
    (nc, sch), env = build_core_program()
    finish_program(nc, sch)
    in_maps = [make_inputs_per_core(c, full) for c in range(8)]
    res = run_bass_kernel_spmd(nc, in_maps, core_ids=list(range(8)))
    out = np.stack([r["out"] for r in res.results], axis=0)
    return out


# version 31
# speedup vs baseline: 1.0117x; 1.0117x over previous
import numpy as np
import ml_dtypes
import concourse.bass as bass
import concourse.mybir as mybir
from concourse.bass_utils import run_bass_kernel_spmd
from concourse.dve_op import DveOp
from concourse.dve_spec import Spec, Src0, Src1, C0, C1, C2, C3, sq, lower
import concourse.dve_ops as dve_ops

F32 = mybir.dt.float32
BF16 = mybir.dt.bfloat16
U8 = mybir.dt.uint8
AX = mybir.AxisListType
AOP = mybir.AluOpType
ACTF = mybir.ActivationFunctionType

B, L, D = 8, 2048, 1024
H, DQH, DVH = 4, 128, 256
DK, DV = 512, 1024
I_FFN = 2816
CHUNK = 128
NCH = L // CHUNK
NT = L // 128
EPS_LN = 1e-5
EPS_RMS = 1e-5

def _mk_qlo():
    a = sq(Src0)
    return Spec(body=((a > C0) + (a > C1)) + ((a > C2) + (a > C3)))

def _qlo_ref(in0, in1, s0, s1, imm2):
    a = in0.astype(np.float64) ** 2
    s3 = np.asarray(in1).reshape(in0.shape[0], -1)[:, :1].astype(np.float64)
    return ((a > s0).astype(np.float64) + (a > s1) + (a > imm2) + (a > s3))

def _mk_qhi():
    a = sq(Src0)
    h = ((a > C0) + (a > C1)) + ((a > C2) + (a > C2))
    return Spec(body=h + h)

def _qhi_ref(in0, in1, s0, s1, imm2):
    a = in0.astype(np.float64) ** 2
    return 2.0 * ((a > s0).astype(np.float64) + (a > s1) + 2.0 * (a > imm2))

def _register_op(name, spec, ref, slot):
    for op in dve_ops.OPS:
        if op.name == name:
            return op
    sha = {v: lower(spec, ver=v, perf_en=False, name=name).sha(v) for v in ("v3",)}
    op = DveOp(name, spec, subdim=False, uops_sha=sha, reference=ref)
    old = dve_ops.OPS[slot]
    dve_ops.OPS[slot] = op
    dve_ops._SUB_OPCODE_FOR_NAME[name] = dve_ops._CUSTOM_DVE_ROW_BASE + slot
    dve_ops.CUSTOM_DVE_SPECS.pop(old.name, None)
    dve_ops.CUSTOM_DVE_SPECS[name] = spec
    dve_ops.CUSTOM_DVE_REFS.pop(old.name, None)
    dve_ops.CUSTOM_DVE_REFS[name] = ref
    return op

QLO = _register_op("ANT_QLO", _mk_qlo(), _qlo_ref, len(dve_ops.OPS) - 1)
QHI = _register_op("ANT_QHI", _mk_qhi(), _qhi_ref, len(dve_ops.OPS) - 2)

ENGINES = ("sync", "scalar", "vector", "tensor", "gpsimd")
DMAQ = ("sync", "scalar", "gpsimd")

class Sched:
    def __init__(self):
        self.entries = []
        self.cnt = {e: 0 for e in ENGINES}
        self.dcnt = {e: 0 for e in DMAQ}
        self.flushed = {e: 0 for e in DMAQ}
        self.last_w = {}
        self.readers = {}
        self.flushed_vals = {e: {} for e in DMAQ}

    def _flush(self, q, k):
        if k <= self.flushed[q]:
            return self.flushed_vals[q][k]
        tot = self.dcnt[q]
        self.entries.append((q, None, [(q, "d", 16 * tot, "flush")], 0))
        for i in range(self.flushed[q] + 1, tot + 1):
            self.flushed_vals[q][i] = 16 * tot
        self.flushed[q] = tot
        return 16 * tot

    def op(self, eng, emit, reads=(), writes=(), inc=1):
        is_dma = inc >= 16
        deps = []
        for b in reads:
            w = self.last_w.get(b)
            if w is not None:
                deps.append(w + ("raw",))
            if b.startswith("ps"):
                for r in self.readers.get(b, ()):
                    if r[0] != eng:
                        deps.append(r + ("rr",))
        for b in writes:
            w = self.last_w.get(b)
            if w is not None:
                deps.append(w + ("waw",))
            deps.extend(r + ("war",) for r in self.readers.get(b, ()))
        waits = []
        for (weng, wstream, wcnt, kind) in deps:
            if wstream == "d":
                val = self._flush(weng, wcnt)
                waits.append((weng, "d", val, kind))
            else:
                waits.append((weng, "c", wcnt, kind))
        if is_dma:
            self.dcnt[eng] += 1
            done = (eng, "d", self.dcnt[eng])
        else:
            self.cnt[eng] += 1
            done = (eng, "c", self.cnt[eng])
        for b in reads:
            self.readers.setdefault(b, []).append(done)
        for b in writes:
            self.last_w[b] = done
            self.readers[b] = []
        self.entries.append((eng, emit, waits, inc))
        return done

    def barrier(self):
        waitsets = {}
        for e in ENGINES:
            snap = [(o, "c", self.cnt[o], "bar") for o in ENGINES if o != e and self.cnt[o]]
            for q in DMAQ:
                if self.dcnt[q]:
                    val = self._flush(q, self.dcnt[q])
                    if q != e:
                        snap.append((q, "d", val, "bar"))
            waitsets[e] = snap
        for e in ENGINES:
            self.entries.append((e, None, waitsets[e], 0))

    def replay(self, eng_name, handle, sems, max_entries=None):
        waited = {}
        ents = self.entries if max_entries is None else self.entries[:max_entries]
        for eng, emit, waits, inc in ents:
            if eng != eng_name:
                continue
            for (weng, wstream, wval, kind) in waits:
                if weng == eng_name and wstream == "c" and kind == "waw":
                    continue
                key = (weng, wstream)
                if wval > waited.get(key, 0):
                    handle.wait_ge(sems[key], wval)
                    waited[key] = wval
            if emit is not None:
                ins = emit(handle)
                ins.then_inc(sems[(eng_name, "d" if inc >= 16 else "c")], inc)


def ceil_div(a, b):
    return (a + b - 1) // b


def build_core_program(debug_taps=()):
    nc = bass.Bass("TRN2", target_bir_lowering=False)
    sch = Sched()

    def dram_in(name, shape, dtype=F32):
        return nc.dram_tensor(name, shape, dtype, kind="ExternalInput").ap()

    def scratch(name, shape, dtype):
        kind = "ExternalOutput" if name in debug_taps else "Internal"
        return nc.dram_tensor(name, shape, dtype, kind=kind).ap()

    x_in = dram_in("x", [L, D])
    convw = dram_in("convw", [128, 32])
    wq_in = dram_in("wq", [D, DK])
    wk_in = dram_in("wk", [D, DK])
    wv_in = dram_in("wv", [D, DV])
    wbt_in = dram_in("wbt", [128, 32])
    wg_in = dram_in("wg", [D, DV])
    wo_in = dram_in("wo", [DV, D])
    wgate_in = dram_in("wgate", [I_FFN, D])
    wup_in = dram_in("wup", [I_FFN, D])
    wdown_in = dram_in("wdown", [D, I_FFN])
    nwrep4 = dram_in("nwrep4", [128, 1024])
    identb = dram_in("identb", [128, 128], BF16)
    onesb = dram_in("onesb", [128, 128], BF16)
    maskag = dram_in("maskag", [128, 1024], BF16)
    maskaq = dram_in("maskaq", [128, 512], BF16)
    selh_in = dram_in("selh", [4, 512], BF16)

    out_t = nc.dram_tensor("out", [L, D], F32, kind="ExternalOutput").ap()

    xnfm_s = scratch("xnfm_s", [D, L], BF16)
    xq_s = scratch("xq_s", [L, D], BF16)
    wgqT_s = scratch("wgqT_s", [D, I_FFN], BF16)
    wuqT_s = scratch("wuqT_s", [D, I_FFN], BF16)
    wdqT_s = scratch("wdqT_s", [I_FFN, D], BF16)
    ap_s = scratch("ap_s", [L, DV], BF16)

    def psum(name, shape, dtype):
        t = nc.psum_tensor(name, shape, dtype)
        h = t.__enter__()
        return h

    psS = psum("psS", [128, 1024], F32)
    psF = psum("psF", [128, 1024], F32)
    psV = psum("psV", [128, 1024], F32)
    psT = psum("psT", [128, 1024], BF16)
    ps7 = psum("ps7", [128, 512], F32)

    sb_stack = []

    def sbuf(name, shape, dtype):
        t = nc.sbuf_tensor(name, shape, dtype)
        h = t.__enter__()
        sb_stack.append(t)
        return h

    cw_sb = sbuf("cw_sb", [128, 32], F32)
    nw_sb = sbuf("nw_sb", [128, 1024], BF16)
    idn_sb = sbuf("idn_sb", [128, 128], BF16)
    ones_sb = sbuf("ones_sb", [128, 128], BF16)
    mag_sb = sbuf("mag_sb", [128, 1024], BF16)
    maq_sb = sbuf("maq_sb", [128, 512], BF16)
    qc3_sb = sbuf("qc3_sb", [128, 1], F32)
    selh_sb = sbuf("selh_sb", [4, 512], BF16)

    def S(e):
        return sch.op(*e[0:2], **e[2]) if False else None

    sch.op("sync", lambda s: s.dma_start(out=cw_sb[:], in_=convw[:]), writes=["cw_sb"], inc=16)
    sch.op("gpsimd", lambda g: g.dma_start(out=nw_sb[:], in_=nwrep4[:]), writes=["nw_sb"], inc=16)
    sch.op("sync", lambda s: s.dma_start(out=idn_sb[:], in_=identb[:]), writes=["idn_sb"], inc=16)
    sch.op("sync", lambda s: s.dma_start(out=ones_sb[:], in_=onesb[:]), writes=["ones_sb"], inc=16)
    sch.op("sync", lambda s: s.dma_start(out=mag_sb[:], in_=maskag[:]), writes=["mag_sb"], inc=16)
    sch.op("sync", lambda s: s.dma_start(out=maq_sb[:], in_=maskaq[:]), writes=["maq_sb"], inc=16)
    sch.op("vector", lambda v: v.memset(qc3_sb[:], 3.0625), writes=["qc3"])
    sch.op("sync", lambda s: s.dma_start(out=selh_sb[:], in_=selh_in[:]), writes=["selh"], inc=16)

    def quant_stage1(src_ap, src_key, n, bufs):
        kp = bufs["kp"]
        nb = n // 16
        am_ap, rin_ap, sc_ap = bufs["am"][:, 0:nb], bufs["rinv"][:, 0:nb], bufs["sch"][:, 0:nb]
        src3 = src_ap.rearrange("p (b c) -> p b c", c=16)
        sch.op("vector",
               lambda v: v.reduce_max(am_ap, src3, axis=AX.X, apply_absolute_value=True),
               reads=[src_key], writes=[kp + "am"])
        sch.op("vector",
               lambda v: v.tensor_scalar(sc_ap, am_ap, 1.0 / 6.0, 1e-12,
                                         op0=AOP.mult, op1=AOP.max),
               reads=[kp + "am"], writes=[kp + "sc"])
        sch.op("vector",
               lambda v: v.reciprocal(rin_ap, sc_ap),
               reads=[kp + "sc"], writes=[kp + "rinv"])
        sch.op("vector",
               lambda v: v.tensor_scalar_mul(sc_ap, sc_ap, 0.5),
               reads=[kp + "sc"], writes=[kp + "sc"])

    def quant_stage2(src_ap, src_key, n, bufs):
        kp = bufs["kp"]
        nb = n // 16
        rin_ap = bufs["rinv"][:, 0:nb]
        src3 = src_ap.rearrange("p (b c) -> p b c", c=16)
        rinb = rin_ap.unsqueeze(2).broadcast_to([128, nb, 16])
        t3 = bufs["t"][:, 0:n].rearrange("p (b c) -> p b c", c=16)
        sch.op("gpsimd",
               lambda g: g.tensor_tensor(t3, src3, rinb, op=AOP.mult),
               reads=[src_key, kp + "rinv"], writes=[kp + "t"])

    def quant_stage3(n, bufs):
        kp = bufs["kp"]
        t_ap = bufs["t"][:, 0:n]
        s_ap, h_ap, sg_ap = bufs["s"][:, 0:n], bufs["h"][:, 0:n], bufs["sg"][:, 0:n]
        sch.op("vector",
               lambda v: v._custom_dve(QLO, out=s_ap, in0=t_ap, in1=qc3_sb[:],
                                       s0=0.0625, s1=0.5625, imm2=1.5625),
               reads=[kp + "t", "qc3"], writes=[kp + "s"])
        sch.op("vector",
               lambda v: v._custom_dve(QHI, out=h_ap, in0=t_ap, s0=6.25, s1=12.25, imm2=25.0),
               reads=[kp + "t"], writes=[kp + "h"])
        sch.op("scalar",
               lambda sc: sc.activation(sg_ap, t_ap, ACTF.Sign),
               reads=[kp + "t"], writes=[kp + "sg"])

    def quant_stage4(n, bufs):
        kp = bufs["kp"]
        nb = n // 16
        t_ap = bufs["t"][:, 0:n]
        s_ap, h_ap = bufs["s"][:, 0:n], bufs["h"][:, 0:n]
        sc_ap = bufs["sch"][:, 0:nb]
        sch.op("vector",
               lambda v: v.tensor_add(s_ap, h_ap, s_ap),
               reads=[kp + "h", kp + "s"], writes=[kp + "s"])
        scb = sc_ap.unsqueeze(2).broadcast_to([128, nb, 16])
        s3 = s_ap.rearrange("p (b c) -> p b c", c=16)
        t3 = t_ap.rearrange("p (b c) -> p b c", c=16)
        sch.op("gpsimd",
               lambda g: g.tensor_tensor(t3, s3, scb, op=AOP.mult),
               reads=[kp + "s", kp + "sc"], writes=[kp + "t"])

    def quant_stage5(dst_ap, dst_key, n, bufs, eng):
        kp = bufs["kp"]
        t_ap, sg_ap = bufs["t"][:, 0:n], bufs["sg"][:, 0:n]
        sch.op(eng,
               lambda e: e.tensor_tensor(dst_ap, t_ap, sg_ap, op=AOP.mult),
               reads=[kp + "t", kp + "sg"], writes=[dst_key])

    def emit_quant(tag, src_ap, src_key, dst_ap, dst_key, n, bufs):
        kp = bufs["kp"]
        nb = n // 16
        t_ap, s_ap, h_ap = bufs["t"][:, 0:n], bufs["s"][:, 0:n], bufs["h"][:, 0:n]
        sg_ap = bufs["sg"][:, 0:n]
        am_ap, rin_ap, sc_ap = bufs["am"][:, 0:nb], bufs["rinv"][:, 0:nb], bufs["sch"][:, 0:nb]
        src3 = src_ap.rearrange("p (b c) -> p b c", c=16)
        sch.op("vector",
               lambda v: v.reduce_max(am_ap, src3, axis=AX.X, apply_absolute_value=True),
               reads=[src_key], writes=[kp + "am"])
        sch.op("vector",
               lambda v: v.tensor_scalar(sc_ap, am_ap, 1.0 / 6.0, 1e-12,
                                         op0=AOP.mult, op1=AOP.max),
               reads=[kp + "am"], writes=[kp + "sc"])
        sch.op("vector",
               lambda v: v.reciprocal(rin_ap, sc_ap),
               reads=[kp + "sc"], writes=[kp + "rinv"])
        sch.op("vector",
               lambda v: v.tensor_scalar_mul(sc_ap, sc_ap, 0.5),
               reads=[kp + "sc"], writes=[kp + "sc"])
        rinb = rin_ap.unsqueeze(2).broadcast_to([128, nb, 16])
        t3 = t_ap.rearrange("p (b c) -> p b c", c=16)
        sch.op("gpsimd",
               lambda g: g.tensor_tensor(t3, src3, rinb, op=AOP.mult),
               reads=[src_key, kp + "rinv"], writes=[kp + "t"])
        sch.op("vector",
               lambda v: v._custom_dve(QLO, out=s_ap, in0=t_ap, in1=qc3_sb[:],
                                       s0=0.0625, s1=0.5625, imm2=1.5625),
               reads=[kp + "t", "qc3"], writes=[kp + "s"])
        sch.op("vector",
               lambda v: v._custom_dve(QHI, out=h_ap, in0=t_ap, s0=6.25, s1=12.25, imm2=25.0),
               reads=[kp + "t"], writes=[kp + "h"])
        sch.op("scalar",
               lambda sc: sc.activation(sg_ap, t_ap, ACTF.Sign),
               reads=[kp + "t"], writes=[kp + "sg"])
        sch.op("vector",
               lambda v: v.tensor_add(s_ap, h_ap, s_ap),
               reads=[kp + "h", kp + "s"], writes=[kp + "s"])
        scb = sc_ap.unsqueeze(2).broadcast_to([128, nb, 16])
        s3 = s_ap.rearrange("p (b c) -> p b c", c=16)
        sch.op("gpsimd",
               lambda g: g.tensor_tensor(t3, s3, scb, op=AOP.mult),
               reads=[kp + "s", kp + "sc"], writes=[kp + "t"])
        sch.op("gpsimd",
               lambda g: g.tensor_tensor(dst_ap, t_ap, sg_ap, op=AOP.mult),
               reads=[kp + "t", kp + "sg"], writes=[dst_key])

    xt = [sbuf(f"xt{i}", [128, D], F32) for i in range(2)]
    xnt = [sbuf("xnt0", [128, D], F32)]
    xnbt = [sbuf(f"xnbt{i}", [128, D], BF16) for i in range(2)]
    xqt = [sbuf("xqt0", [128, D], BF16)]
    stat_sx = sbuf("stat_sx", [128, NT], F32)
    stat_sq = sbuf("stat_sq", [128, NT], F32)
    stat_m = sbuf("stat_m", [128, NT], F32)
    stat_iv = sbuf("stat_iv", [128, NT], F32)
    qb_t = sbuf("qb_t", [128, D], F32)
    qb_h = sbuf("qb_h", [128, D], F32)
    qb_sg = sbuf("qb_sg", [128, D], BF16)
    qb_qs = sbuf("qb_qs", [128, 1024], BF16)
    qb_qh = sbuf("qb_qh", [128, 1024], BF16)
    qb_am = sbuf("qb_am", [128, 64], F32)
    qb_ri = sbuf("qb_ri", [128, 64], F32)
    qb_sc = sbuf("qb_sc", [128, 64], F32)
    qset0 = {"t": qb_t[:], "s": qb_qs[:], "h": qb_qh[:], "sg": qb_sg[:],
             "am": qb_am[:], "rinv": qb_ri[:], "sch": qb_sc[:], "kp": "qb."}



    aw_f32_g = nc.sbuf_tensor("aw_f32", [128, 14080], F32)
    aw_f32 = aw_f32_g.__enter__()
    aw_bf_g = nc.sbuf_tensor("aw_bf", [128, 14336], BF16)
    aw_bf = aw_bf_g.__enter__()
    aw_ac_g = nc.sbuf_tensor("aw_ac", [128, 22528], BF16)
    aw_ac = aw_ac_g.__enter__()

    def _aw_qset(s):
        b0 = 3072 * s
        f0 = 11264 + 192 * s
        return {"t": aw_f32[:, 7168 + 1024 * s:8192 + 1024 * s],
                "s": aw_bf[:, b0:b0 + 1024], "h": aw_bf[:, b0 + 1024:b0 + 2048],
                "sg": aw_bf[:, b0 + 2048:b0 + 3072],
                "am": aw_f32[:, f0:f0 + 64], "rinv": aw_f32[:, f0 + 64:f0 + 128],
                "sch": aw_f32[:, f0 + 128:f0 + 192], "kp": f"a{s + 1}."}
    QSETS_AW = [qset0] + [_aw_qset(s) for s in range(4)]
    qset_c0 = {"t": qb_t[:, 0:512], "s": qb_qs[:, 0:512], "h": qb_qh[:, 0:512],
               "sg": qb_sg[:, 0:512], "am": qb_am[:, 0:32], "rinv": qb_ri[:, 0:32],
               "sch": qb_sc[:, 0:32], "kp": "qb."}
    qset_c1 = {"t": qb_t[:, 512:1024], "s": qb_qs[:, 512:1024], "h": qb_qh[:, 512:1024],
               "sg": qb_sg[:, 512:1024], "am": qb_am[:, 32:64], "rinv": qb_ri[:, 32:64],
               "sch": qb_sc[:, 32:64], "kp": "qc."}
    QSETS_C = [qset_c0, qset_c1]

    ld_pool = [(xt[0][:], "xt0"), (xt[1][:], "xt1")] + [
        (aw_f32[:, 1024 * j:1024 * (j + 1)], f"awl{j}") for j in range(4)]
    xn_pool = [(xnt[0][:], "xnt0")] + [
        (aw_f32[:, 4096 + 1024 * j:5120 + 1024 * j], f"awx{j}") for j in range(3)]
    out_pool = [(xqt[0][:], "xqt0"), (aw_bf[:, 12288:13312], "awo1"),
                (aw_bf[:, 13312:14336], "awo2")]
    sq_pool = [(aw_f32[:, 12032:13056], "awsq0"), (aw_f32[:, 13056:14080], "awsq1")]

    wjobs = []
    for (wsrc, wdstT, rows, cols, wtag) in ((wgate_in, wgqT_s, I_FFN, D, "wgT"),
                                            (wup_in, wuqT_s, I_FFN, D, "wuT"),
                                            (wdown_in, wdqT_s, D, I_FFN, "wdT")):
        for wi in range(rows // 128):
            for c0 in range(0, cols, 1024):
                wjobs.append((wsrc, wdstT, wi, c0, min(1024, cols - c0), wtag))
    tiles = [("A", i) for i in range(NT)] + [("W", w) for w in wjobs]
    WLAST = {}
    for gg, (kind_, info_) in enumerate(tiles):
        if kind_ == "W":
            WLAST[info_[5]] = gg

    def emit_stage(st, g):
        kind, info = tiles[g]
        la, kl = ld_pool[g % 6]
        qs = QSETS_AW[g % 5]
        oa, ko = out_pool[g % 3]
        oeng = "gpsimd" if g % 2 else "vector"
        if kind == "A":
            i = info
            xna, kxn = xn_pool[g % 4]
            xnba, kxnb = xnbt[g % 2][:], f"xnbt{g % 2}"
            sqa, ksq = sq_pool[g % 2]
            stp = f"st{i}."
            if st == 0:
                sch.op("scalar", lambda s: s.dma_start(out=la[:], in_=x_in[i * 128:(i + 1) * 128, :]),
                       writes=[kl], inc=16)
            elif st == 1:
                sch.op("vector", lambda v: v.reduce_sum(stat_sx[:, i:i + 1], la[:], axis=AX.X),
                       reads=[kl], writes=[stp + "sx"])
                sch.op("scalar", lambda sc: sc.activation(sqa[:], la[:], ACTF.Square,
                                                          accum_out=stat_sq[:, i:i + 1]),
                       reads=[kl], writes=[stp + "sq", ksq])
            elif st == 2:
                sch.op("vector", lambda v: v.tensor_scalar_mul(stat_m[:, i:i + 1], stat_sx[:, i:i + 1], 1.0 / 1024.0),
                       reads=[stp + "sx"], writes=[stp + "m"])
                sch.op("vector", lambda v: v.tensor_scalar_mul(stat_iv[:, i:i + 1], stat_sq[:, i:i + 1], 1.0 / 1024.0),
                       reads=[stp + "sq"], writes=[stp + "iv"])
                sch.op("vector", lambda v: v.tensor_mul(stat_sq[:, i:i + 1],
                                                        stat_m[:, i:i + 1], stat_m[:, i:i + 1]),
                       reads=[stp + "m", stp + "sq"], writes=[stp + "sq"])
                sch.op("vector", lambda v: v.tensor_sub(stat_iv[:, i:i + 1], stat_iv[:, i:i + 1],
                                                        stat_sq[:, i:i + 1]),
                       reads=[stp + "iv", stp + "sq"], writes=[stp + "iv"])
                sch.op("vector", lambda v: v.tensor_scalar_add(stat_iv[:, i:i + 1], stat_iv[:, i:i + 1],
                                                               EPS_LN),
                       reads=[stp + "iv"], writes=[stp + "iv"])
                sch.op("scalar", lambda sc: sc.activation(stat_iv[:, i:i + 1], stat_iv[:, i:i + 1],
                                                          ACTF.Sqrt),
                       reads=[stp + "iv"], writes=[stp + "iv"])
                sch.op("vector", lambda v: v.reciprocal(stat_iv[:, i:i + 1], stat_iv[:, i:i + 1]),
                       reads=[stp + "iv"], writes=[stp + "iv"])
                sch.op("vector", lambda v: v.tensor_scalar(
                           xna[:], la[:], stat_m[:, i:i + 1], stat_iv[:, i:i + 1],
                           op0=AOP.subtract, op1=AOP.mult),
                       reads=[kl, stp + "m", stp + "iv"], writes=[kxn])
            elif st == 3:
                quant_stage1(xna, kxn, D, qs)
                sch.op("scalar", lambda sc: sc.activation(xnba, xna, ACTF.Copy),
                       reads=[kxn], writes=[kxnb])
            elif st == 4:
                quant_stage2(xna, kxn, D, qs)
                for j in range(8):
                    sch.op("tensor", lambda t, j=j: t.transpose(
                               psT[:, j * 128:(j + 1) * 128], xnba[:, j * 128:(j + 1) * 128],
                               idn_sb[:]),
                           reads=[kxnb, "idn_sb"], writes=["psT"])
                for j in range(8):
                    sch.op("scalar", lambda sc, j=j: sc.activation(
                               aw_ac[:, j * 2048 + i * 128:j * 2048 + (i + 1) * 128],
                               psT[:, j * 128:(j + 1) * 128], ACTF.Copy),
                           reads=["psT"], writes=["acc"])
                if i == NT - 1:
                    for kc in range(8):
                        sch.op("sync", lambda s, kc=kc: s.dma_start(
                                   out=xnfm_s[kc * 128:(kc + 1) * 128, :],
                                   in_=aw_ac[:, kc * 2048:(kc + 1) * 2048]),
                               reads=["acc"], writes=["xnfm"], inc=16)
            elif st == 5:
                quant_stage3(D, qs)
            elif st == 6:
                quant_stage4(D, qs)
            elif st == 7:
                quant_stage5(oa[:, 0:D], ko, D, qs, oeng)
                sch.op("sync", lambda s: s.dma_start(out=xq_s[i * 128:(i + 1) * 128, :], in_=oa[:, 0:D]),
                       reads=[ko], writes=["xq_s"], inc=16)
        else:
            wsrc, wdstT, wi, c0, cw, wtag = info
            if st == 0:
                sch.op("scalar", lambda s: s.dma_start(out=la[:, 0:cw],
                                                       in_=wsrc[wi * 128:(wi + 1) * 128, c0:c0 + cw]),
                       writes=[kl], inc=16)
            elif st == 3:
                quant_stage1(la[:, 0:cw], kl, cw, qs)
            elif st == 4:
                quant_stage2(la[:, 0:cw], kl, cw, qs)
            elif st == 5:
                quant_stage3(cw, qs)
            elif st == 6:
                quant_stage4(cw, qs)
            elif st == 7:
                quant_stage5(oa[:, 0:cw], ko, cw, qs, oeng)
            elif st == 8:
                for j in range(cw // 128):
                    sch.op("tensor", lambda t, j=j: t.transpose(
                               psT[:, j * 128:(j + 1) * 128], oa[:, j * 128:(j + 1) * 128],
                               idn_sb[:]),
                           reads=[ko, "idn_sb"], writes=["psT"])
                for j in range(cw // 128):
                    if wtag == "wdT":
                        off = (c0 // 128 + j) * 1024 + wi * 128
                    else:
                        off = j * 2816 + wi * 128
                    sch.op("scalar", lambda s, j=j, off=off: s.activation(
                               aw_ac[:, off:off + 128], psT[:, j * 128:(j + 1) * 128], ACTF.Copy),
                           reads=["psT"], writes=["acc"])
                if g == WLAST[wtag]:
                    ncols = 1024 if wtag == "wdT" else 2816
                    for ch in range((I_FFN if wtag == "wdT" else D) // 128):
                        sch.op("sync", lambda s, ch=ch, ncols=ncols: s.dma_start(
                                   out=wdstT[ch * 128:(ch + 1) * 128, :],
                                   in_=aw_ac[:, ch * ncols:(ch + 1) * ncols]),
                               reads=["acc"], writes=[wtag], inc=16)

    NST = 10
    for step in range(len(tiles) + NST - 1):
        for st in range(NST - 1, -1, -1):
            g = step - st
            if 0 <= g < len(tiles):
                emit_stage(st, g)

    sch.barrier()
    sch.marks = {"AW": len(sch.entries)}
    aw_ac_g.__exit__(None, None, None)
    aw_bf_g.__exit__(None, None, None)
    aw_f32_g.__exit__(None, None, None)

    h_sb = sbuf("h_sb", [128, 8 * 2048], BF16)
    qkk = sbuf("qkk", [128, 3 * 8192], BF16)
    wvgo = sbuf("wvgo", [128, 2 * 8192], BF16)
    wqk = sbuf("wqk", [128, 8192], BF16)
    dp = sbuf("dp", [128, 8192], BF16)
    dp2 = sbuf("dp2", [128, 4096], BF16)
    hpre = sbuf("hpre", [128, 2048], F32)
    betab = sbuf("betab", [4, 2048], BF16)
    btm = sbuf("btm", [128, 64], F32)
    btmn = sbuf("btmn", [128, 64], F32)
    wbt_sb = sbuf("wbt_sb", [128, 32], BF16)
    Sbf = sbuf("Sbf", [128, 1024], BF16)

    QN, KN, KB = 0, 8192, 16384
    AG0 = 0
    PW = {2: 1024, 4: 2048, 8: 3072, 16: 4096, 32: 5120}
    G64S, B1S = 6144, 6656
    B2S, B3S = 7168, 7680
    AQT, KCT = 0, 512
    U0, UA, UB = 1024, 2048, 3072
    vc_ap, vc_key = xnbt[0][:], "xnbt0"
    u0n_ap, u0n_key = xnbt[1][:], "xnbt1"
    uc_ap, uc_key = xqt[0][:], "xqt0"
    apre_ap, apre_key = qb_sg[:], "qb.sg"



    for kc in range(8):
        sch.op("gpsimd", lambda g, kc=kc: g.dma_start(out=wqk[:, kc * 512:(kc + 1) * 512],
                                                      in_=wq_in[kc * 128:(kc + 1) * 128, :]),
               writes=["wqk"], inc=16)
        sch.op("gpsimd", lambda g, kc=kc: g.dma_start(out=wqk[:, 4096 + kc * 512:4096 + (kc + 1) * 512],
                                                      in_=wk_in[kc * 128:(kc + 1) * 128, :]),
               writes=["wqk"], inc=16)
        sch.op("gpsimd", lambda g, kc=kc: g.dma_start(out=wvgo[:, kc * 1024:(kc + 1) * 1024],
                                                      in_=wv_in[kc * 128:(kc + 1) * 128, :]),
               writes=["wv"], inc=16)
        sch.op("gpsimd", lambda g, kc=kc: g.dma_start(out=wvgo[:, 8192 + kc * 1024:8192 + (kc + 1) * 1024],
                                                      in_=wg_in[kc * 128:(kc + 1) * 128, :]),
               writes=["wg"], inc=16)
    sch.op("gpsimd", lambda g: g.dma_start(out=wbt_sb[:], in_=wbt_in[:]), writes=["wbt"], inc=16)

    xnfm = [(qkk[:, 0:2048], "qkkA"), (qkk[:, 2048:4096], "qkkB")]
    for kc in range(8):
        xa, kx = xnfm[kc % 2]
        sch.op("sync", lambda s, kc=kc, xa=xa: s.dma_start(
                   out=xa, in_=xnfm_s[kc * 128:(kc + 1) * 128, :]),
               reads=["xnfm"], writes=[kx], inc=16)
        sch.op("vector", lambda v, kc=kc, xa=xa: v.tensor_scalar_mul(
                   hpre[:], xa, cw_sb[:, kc * 4 + 3:kc * 4 + 4]),
               reads=[kx, "cw_sb"], writes=["hpre"])
        for tap in (2, 1, 0):
            sh = 3 - tap
            sch.op("vector", lambda v, kc=kc, xa=xa, tap=tap, sh=sh: v.scalar_tensor_tensor(
                       hpre[:, sh:2048], xa[:, 0:2048 - sh], cw_sb[:, kc * 4 + tap:kc * 4 + tap + 1],
                       hpre[:, sh:2048], op0=AOP.mult, op1=AOP.add),
                   reads=[kx, "cw_sb", "hpre"], writes=["hpre"])
        sch.op("scalar", lambda s, kc=kc: s.activation(h_sb[:, kc * 2048:(kc + 1) * 2048], hpre[:],
                                                       ACTF.Silu),
               reads=["hpre"], writes=["h"])

    if "h_s" in debug_taps:
        h_s = scratch("h_s", [128, 8 * 2048], BF16)
        sch.op("sync", lambda sy: sy.dma_start(out=h_s[:], in_=h_sb[:]),
               reads=["h"], writes=["h_s"], inc=16)

    for g4 in range(4):
        for kc in range(8):
            sch.op("tensor", lambda t, g4=g4, kc=kc: t.matmul(
                       ps7[0:4, 0:512], wbt_sb[:, kc * 4:(kc + 1) * 4],
                       h_sb[:, kc * 2048 + g4 * 512: kc * 2048 + (g4 + 1) * 512],
                       start=(kc == 0), stop=(kc == 7)),
                   reads=["wbt", "h"], writes=["ps7"])
        sch.op("scalar", lambda s, g4=g4: s.activation(betab[0:4, g4 * 512:(g4 + 1) * 512],
                                                       ps7[0:4, 0:512], ACTF.Sigmoid),
               reads=["ps7"], writes=["betab"])
    for c in range(NCH):
        sch.op("tensor", lambda t, c=c: t.transpose(psT[:, c * 4:(c + 1) * 4],
                                                    betab[0:4, c * 128:(c + 1) * 128],
                                                    idn_sb[0:4, 0:4]),
               reads=["betab", "idn_sb"], writes=["psT"])
    sch.op("vector", lambda v: v.tensor_copy(btm[:], psT[:, 0:64]),
           reads=["psT"], writes=["btm"])
    sch.op("vector", lambda v: v.tensor_scalar_mul(btmn[:], btm[:], -1.0),
           reads=["btm"], writes=["btmn"])

    pgctr = [0]
    pend = [None]

    def l2tail(info):
        (dst, sga, sgk, sqa, sqk, ria, rik, psq, psqk, fk) = info
        sch.op("tensor", lambda t, sga=sga, psq=psq: t.matmul(psq, ones_sb[:], sga,
                                                              start=True, stop=True),
               reads=[sgk, "ones_sb"], writes=[psqk])
        sch.op("scalar", lambda s, psq=psq, sqa=sqa: s.activation(sqa, psq, ACTF.Sqrt),
               reads=[psqk], writes=[sqk])
        sch.op("vector", lambda v, sqa=sqa: v.tensor_scalar_max(sqa, sqa, 1e-12),
               reads=[sqk], writes=[sqk])
        sch.op("vector", lambda v, sqa=sqa, ria=ria: v.reciprocal(ria, sqa),
               reads=[sqk], writes=[rik])
        sch.op("vector", lambda v, dst=dst, ria=ria, fk=fk: v.tensor_mul(
                   qkk[:, dst:dst + 512], qkk[:, dst:dst + 512], ria),
               reads=[fk, rik], writes=[fk])

    for which, w0, dst0 in (("q", 0, QN), ("k", 4096, KN)):
        for hh in range(H):
            for g4 in range(4):
                par = pgctr[0] % 2
                fk = f"qkg{pgctr[0] % 4}"
                pgctr[0] += 1
                pso = par * 512
                psk = f"psF{par}"
                psq, psqk = (ps7[:, 0:512], "ps7") if par == 0 else (psS[:, 0:512], "psS")
                sga, sgk = qb_sg[:, pso:pso + 512], f"qb.sg{par}"
                sqa, sqk = qb_t[:, pso:pso + 512], f"qb.t{par}"
                ria, rik = qb_h[:, pso:pso + 512], f"qb.h{par}"
                for kc in range(8):
                    sch.op("tensor", lambda t, hh=hh, g4=g4, kc=kc, w0=w0, pso=pso: t.matmul(
                               psF[:, pso:pso + 512],
                               wqk[:, w0 + kc * 512 + hh * 128: w0 + kc * 512 + (hh + 1) * 128],
                               h_sb[:, kc * 2048 + g4 * 512: kc * 2048 + (g4 + 1) * 512],
                               start=(kc == 0), stop=(kc == 7)),
                           reads=["wqk", "h"], writes=[psk])
                dst = dst0 + hh * 2048 + g4 * 512
                sch.op("scalar", lambda s, dst=dst, pso=pso, fk=fk: s.activation(
                           qkk[:, dst:dst + 512], psF[:, pso:pso + 512], ACTF.Copy),
                       reads=[psk], writes=[fk, "qkkA", "qkkB", "qkkX"])
                sch.op("scalar", lambda s, pso=pso, sga=sga: s.activation(
                           sga, psF[:, pso:pso + 512], ACTF.Square),
                       reads=[psk], writes=[sgk])
                if pend[0] is not None:
                    l2tail(pend[0])
                pend[0] = (dst, sga, sgk, sqa, sqk, ria, rik, psq, psqk, fk)
    l2tail(pend[0])
    sch.op("vector", lambda v: v.memset(stat_m[:, 0:1], 0.0),
           reads=["psF0", "psF1", "qb.sg0", "qb.sg1", "qb.t0", "qb.t1", "qb.h0", "qb.h1",
                  "qkg0", "qkg1", "qkg2", "qkg3"],
           writes=["psF", "qb.sg", "qb.t", "qb.h", "qkk"])
    for hh in range(H):
        for g4 in range(4):
            sch.op("tensor", lambda t, hh=hh, g4=g4: t.matmul(
                       psF[:, 0:512], selh_sb[0:4, hh * 128:(hh + 1) * 128],
                       betab[0:4, g4 * 512:(g4 + 1) * 512],
                       start=True, stop=True),
                   reads=["betab", "selh"], writes=["psF"])
            dst = KB + hh * 2048 + g4 * 512
            src = KN + hh * 2048 + g4 * 512
            sch.op("vector", lambda v, dst=dst, src=src: v.tensor_mul(
                       qkk[:, dst:dst + 512], qkk[:, src:src + 512], psF[:, 0:512]),
                   reads=["qkk", "psF"], writes=["qkk", "qkkX"])

    sch.marks["B1"] = len(sch.entries)
    if "qkk_s" in debug_taps:
        qkk_s = scratch("qkk_s", [128, 3 * 8192], BF16)
        sch.op("sync", lambda sy: sy.dma_start(out=qkk_s[:], in_=qkk[:]),
               reads=["qkk"], writes=["qkk_s"], inc=16)
    if "beta_s" in debug_taps:
        beta_s = scratch("beta_s", [4, 2048], BF16)
        sch.op("sync", lambda sy: sy.dma_start(out=beta_s[:], in_=betab[:]),
               reads=["betab"], writes=["beta_s"], inc=16)

    sch.marks["B3"] = len(sch.entries)
    def mm(out_ap, lhs_ap, rhs_ap, reads, wkey, start, stop=True):
        sch.op("tensor", lambda t: t.matmul(out_ap, lhs_ap, rhs_ap, start=start, stop=stop,
                                            skip_group_check=True),
               reads=list(reads), writes=[wkey])

    def qs_(hh, c):
        return qkk[:, QN + hh * 2048 + c * 128: QN + hh * 2048 + (c + 1) * 128]

    def ks_(hh, c):
        return qkk[:, KN + hh * 2048 + c * 128: KN + hh * 2048 + (c + 1) * 128]

    def kbs_(hh, c):
        return qkk[:, KB + hh * 2048 + c * 128: KB + hh * 2048 + (c + 1) * 128]

    U0, UA = 2048, 3072

    def vcP(c):
        return (vc_ap, vc_key) if c % 2 == 0 else (wqk[:, 0:1024], "wqk")

    def agP(c):
        return (dp[:, AG0:AG0 + 1024], "dp.ag") if c % 2 == 0 else (wqk[:, 1024:2048], "wqk")

    def g64P(c):
        return (dp[:, G64S:G64S + 1024], "dp.g64b1") if c % 2 == 0 else (wqk[:, 2048:3072], "wqk")

    def b23P(c):
        return (dp[:, B2S:B2S + 1024], "dp.b23") if c % 2 == 0 else (wqk[:, 3072:4096], "wqk")

    def aqtP(c):
        return (0, "dp2.aqt") if c % 2 == 0 else (512, "dp2.aqt2")

    def kctP(c):
        return (1024, "dp2.kct") if c % 2 == 0 else (1536, "dp2.kct2")

    def nwgP(c):
        return (hpre[:, 0:1024], "hpre0") if c % 2 == 0 else (hpre[:, 1024:2048], "hpre1")

    def front_stages(c):
        vca, vck = vcP(c)
        aga, agk = agP(c)
        g64a, g64k = g64P(c)
        b23a, b23k = b23P(c)
        aqo, aqk = aqtP(c)
        kco, kck = kctP(c)
        nwa, nwk = nwgP(c)

        def f0():
            for half in range(2):
                for kc in range(8):
                    mm(psV[:, half * 512:(half + 1) * 512],
                       h_sb[:, kc * 2048 + c * 128: kc * 2048 + (c + 1) * 128],
                       wvgo[:, kc * 1024 + half * 512: kc * 1024 + (half + 1) * 512],
                       ["h", "wv"], "psV", start=(kc == 0), stop=(kc == 7))

        def f1():
            sch.op("vector", lambda v: v.tensor_copy(vca, psV[:]), reads=["psV"], writes=[vck])

        def f2():
            for half in range(2):
                for kc in range(8):
                    mm(psV[:, half * 512:(half + 1) * 512],
                       h_sb[:, kc * 2048 + c * 128: kc * 2048 + (c + 1) * 128],
                       wvgo[:, 8192 + kc * 1024 + half * 512: 8192 + kc * 1024 + (half + 1) * 512],
                       ["h", "wg"], "psV", start=(kc == 0), stop=(kc == 7))

        def f3():
            sch.op("scalar", lambda s: s.activation(xnt[0][:], psV[:], ACTF.Silu),
                   reads=["psV"], writes=["xnt0"])
            sch.op("vector", lambda v: v.tensor_mul(nwa, xnt[0][:], nw_sb[:]),
                   reads=["xnt0", "nw_sb"], writes=[nwk])

        def f4():
            for hh in range(H):
                col = hh * 256
                mm(psV[:, col:col + 128], kbs_(hh, c), ks_(hh, c), ["qkk"], "psV",
                   start=(col % 512 == 0))
                mm(psV[:, col + 128:col + 256], ks_(hh, c), kbs_(hh, c), ["qkk"], "psV",
                   start=False)

        def f5():
            sch.op("vector", lambda v: v.tensor_mul(aga, psV[:], mag_sb[:]),
                   reads=["psV", "mag_sb"], writes=[agk])

        def f6():
            for hh in range(H):
                mm(ps7[:, hh * 128:(hh + 1) * 128], ks_(hh, c), qs_(hh, c), ["qkk"], "ps7",
                   start=(hh == 0))
            sch.op("vector", lambda v: v.tensor_mul(dp2[:, aqo:aqo + 512], ps7[:], maq_sb[:]),
                   reads=["ps7", "maq_sb"], writes=[aqk])
            for hh in range(H):
                sch.op("tensor", lambda t, hh=hh: t.transpose(
                           psT[:, hh * 128:(hh + 1) * 128], ks_(hh, c), idn_sb[:]),
                       reads=["qkk", "idn_sb"], writes=["psT"])
            sch.op("vector", lambda v: v.tensor_copy(dp2[:, kco:kco + 512],
                                                     psT[:, 0:512].bitcast(BF16)),
                   reads=["psT"], writes=[kck])

        def mk_lev(lev, prev_ap, prev_key, slot):
            def f():
                for hh in range(H):
                    col = hh * 256
                    mm(psV[:, col:col + 128], prev_ap[:, col + 128:col + 256],
                       prev_ap[:, col:col + 128], [prev_key], "psV",
                       start=(col % 512 == 0))
                    mm(psV[:, col + 128:col + 256], prev_ap[:, col:col + 128],
                       prev_ap[:, col + 128:col + 256], [prev_key], "psV", start=False)
                sch.op("scalar", lambda sc: sc.copy(dp[:, slot:slot + 1024], psV[:]),
                       reads=["psV"], writes=["dp.p%d" % lev])
            return f

        def f12():
            for hh in range(H):
                cG, c32, c2, c4 = hh * 128, PW[32] + hh * 256, PW[2] + hh * 256, PW[4] + hh * 256
                mm(psV[:, cG:cG + 128], dp[:, c32:c32 + 128], dp[:, c32 + 128:c32 + 256],
                   ["dp.p32"], "psV", start=(cG % 512 == 0))
                cB = 512 + hh * 128
                mm(psV[:, cB:cB + 128], idn_sb[:], idn_sb[:], ["idn_sb"], "psV",
                   start=(cB % 512 == 0))
                mm(psV[:, cB:cB + 128], idn_sb[:], dp[:, c2 + 128:c2 + 256], ["dp.p2", "idn_sb"],
                   "psV", start=False)
                mm(psV[:, cB:cB + 128], idn_sb[:], dp[:, c4 + 128:c4 + 256], ["dp.p4", "idn_sb"],
                   "psV", start=False)
                mm(psV[:, cB:cB + 128], dp[:, c2:c2 + 128], dp[:, c4 + 128:c4 + 256],
                   ["dp.p2", "dp.p4"], "psV", start=False)
            sch.op("scalar", lambda sc: sc.copy(g64a, psV[:]), reads=["psV"], writes=[g64k])

        def f13():
            for hh in range(H):
                c8, c16, c32 = PW[8] + hh * 256, PW[16] + hh * 256, PW[32] + hh * 256
                cB2, cB3 = hh * 128, 512 + hh * 128
                g64c = hh * 128
                mm(psV[:, cB2:cB2 + 128], idn_sb[:], idn_sb[:], ["idn_sb"], "psV",
                   start=(cB2 % 512 == 0))
                mm(psV[:, cB2:cB2 + 128], idn_sb[:], dp[:, c8 + 128:c8 + 256], ["dp.p8"], "psV", start=False)
                mm(psV[:, cB2:cB2 + 128], idn_sb[:], dp[:, c16 + 128:c16 + 256], ["dp.p16"], "psV", start=False)
                mm(psV[:, cB2:cB2 + 128], dp[:, c8:c8 + 128], dp[:, c16 + 128:c16 + 256],
                   ["dp.p8", "dp.p16"], "psV", start=False)
                mm(psV[:, cB3:cB3 + 128], idn_sb[:], idn_sb[:], ["idn_sb"], "psV",
                   start=(cB3 % 512 == 0))
                mm(psV[:, cB3:cB3 + 128], idn_sb[:], dp[:, c32 + 128:c32 + 256], ["dp.p32"], "psV", start=False)
                mm(psV[:, cB3:cB3 + 128], idn_sb[:], g64a[:, g64c:g64c + 128], [g64k], "psV", start=False)
                mm(psV[:, cB3:cB3 + 128], dp[:, c32:c32 + 128], g64a[:, g64c:g64c + 128],
                   ["dp.p32", g64k], "psV", start=False)
            sch.op("scalar", lambda sc: sc.copy(b23a, psV[:]), reads=["psV"], writes=[b23k])

        return [f0, f1, f2, f3, f4, f5, f6,
                mk_lev(2, aga, agk, PW[2]),
                mk_lev(4, dp[:, PW[2]:PW[2] + 1024], "dp.p2", PW[4]),
                mk_lev(8, dp[:, PW[4]:PW[4] + 1024], "dp.p4", PW[8]),
                mk_lev(16, dp[:, PW[8]:PW[8] + 1024], "dp.p8", PW[16]),
                mk_lev(32, dp[:, PW[16]:PW[16] + 1024], "dp.p16", PW[32]),
                f12, f13]

    def flow_stages(c):
        vca, vck = vcP(c)
        aga, agk = agP(c)
        g64a, g64k = g64P(c)
        b23a, b23k = b23P(c)
        aqo, aqk = aqtP(c)
        kco, kck = kctP(c)
        nwa, nwk = nwgP(c)

        def l0():
            if c > 0:
                for hh in range(H):
                    col = hh * 256
                    mm(psF[:, col:col + 256], kbs_(hh, c), Sbf[:, hh * 256:(hh + 1) * 256],
                       ["qkk", "Sbf"], "psF", start=(col % 512 == 0))
                for hh in range(H):
                    col = hh * 256
                    sch.op("vector", lambda v, hh=hh, col=col: v.scalar_tensor_tensor(
                               dp2[:, U0 + col:U0 + col + 256], vca[:, col:col + 256],
                               btm[:, c * 4 + hh:c * 4 + hh + 1], psF[:, col:col + 256],
                               op0=AOP.mult, op1=AOP.subtract),
                           reads=[vck, "btm", "psF"], writes=["dp2.u0"])
                    sch.op("vector", lambda v, hh=hh, col=col: v.scalar_tensor_tensor(
                               u0n_ap[:, col:col + 256], vca[:, col:col + 256],
                               btmn[:, c * 4 + hh:c * 4 + hh + 1], psF[:, col:col + 256],
                               op0=AOP.mult, op1=AOP.add),
                           reads=[vck, "btmn", "psF"], writes=[u0n_key])
            else:
                for hh in range(H):
                    col = hh * 256
                    sch.op("vector", lambda v, col=col, hh=hh: v.tensor_scalar_mul(
                               dp2[:, U0 + col:U0 + col + 256], vca[:, col:col + 256],
                               btm[:, c * 4 + hh:c * 4 + hh + 1]),
                           reads=[vck, "btm"], writes=["dp2.u0"])
                    sch.op("vector", lambda v, col=col, hh=hh: v.tensor_scalar_mul(
                               u0n_ap[:, col:col + 256], vca[:, col:col + 256],
                               btmn[:, c * 4 + hh:c * 4 + hh + 1]),
                           reads=[vck, "btmn"], writes=[u0n_key])

        def l1():
            for hh in range(H):
                col = hh * 256
                mm(psF[:, col:col + 256], idn_sb[:], dp2[:, U0 + col:U0 + col + 256],
                   ["dp2.u0", "idn_sb"], "psF", start=(col % 512 == 0))
                mm(psF[:, col:col + 256], aga[:, col + 128:col + 256], u0n_ap[:, col:col + 256],
                   [agk, u0n_key], "psF", start=False)
            sch.op("scalar", lambda sc: sc.copy(dp2[:, UA:UA + 1024], psF[:]),
                   reads=["psF"], writes=["dp2.ua"])

        def mk_b(bmat_ap, bmat_key, boff, src_ap2, skey, dst_ap2, dkey):
            def f():
                for hh in range(H):
                    col = hh * 256
                    mm(psF[:, col:col + 256], bmat_ap[:, boff + hh * 128:boff + (hh + 1) * 128],
                       src_ap2[:, col:col + 256], [bmat_key, skey], "psF",
                       start=(col % 512 == 0))
                sch.op("scalar", lambda sc: sc.copy(dst_ap2, psF[:]),
                       reads=["psF"], writes=[dkey])
            return f

        def l5():
            for hh in range(H):
                col = hh * 256
                if c > 0:
                    mm(psF[:, col:col + 256], qs_(hh, c), Sbf[:, hh * 256:(hh + 1) * 256],
                       ["qkk", "Sbf"], "psF", start=(col % 512 == 0))
                    mm(psF[:, col:col + 256], dp2[:, aqo + hh * 128:aqo + (hh + 1) * 128],
                       dp2[:, U0 + col:U0 + col + 256], [aqk, "dp2.u0"], "psF", start=False)
                else:
                    mm(psF[:, col:col + 256], dp2[:, aqo + hh * 128:aqo + (hh + 1) * 128],
                       dp2[:, U0 + col:U0 + col + 256], [aqk, "dp2.u0"], "psF",
                       start=(col % 512 == 0))
            for hh in range(H):
                col = hh * 256
                mm(psS[:, col:col + 256], dp2[:, kco + hh * 128:kco + (hh + 1) * 128],
                   dp2[:, U0 + col:U0 + col + 256], [kck, "dp2.u0"], "psS",
                   start=(c == 0 and col % 512 == 0))
            sch.op("scalar", lambda sc: sc.copy(Sbf[:], psS[:]), reads=["psS"], writes=["Sbf"])

        def l6():
            sch.op("vector", lambda v: v.tensor_copy(qb_h[:], psF[:]),
                   reads=["psF"], writes=["qb.h"])
            for hh in range(H):
                col = hh * 256
                sch.op("vector", lambda v, hh=hh, col=col: v.tensor_tensor_reduce(
                           out=qb_qs[:, col:col + 256], in0=qb_h[:, col:col + 256],
                           in1=qb_h[:, col:col + 256], scale=1.0, scalar=0.0,
                           op0=AOP.mult, op1=AOP.add, accum_out=stat_sx[:, hh:hh + 1]),
                       reads=["qb.h"], writes=["stat_sx", "qb.s"])
            sch.op("vector", lambda v: v.tensor_scalar(stat_sx[:, 4:8], stat_sx[:, 0:4],
                                                       1.0 / 256.0, EPS_RMS, op0=AOP.mult, op1=AOP.add),
                   reads=["stat_sx"], writes=["stat_sx"])
            sch.op("scalar", lambda s: s.activation(stat_sx[:, 4:8], stat_sx[:, 4:8], ACTF.Sqrt),
                   reads=["stat_sx"], writes=["stat_sx"])
            sch.op("vector", lambda v: v.reciprocal(stat_sx[:, 4:8], stat_sx[:, 4:8]),
                   reads=["stat_sx"], writes=["stat_sx"])

        def l7():
            for hh in range(H):
                col = hh * 256
                sch.op("vector", lambda v, hh=hh, col=col: v.tensor_scalar_mul(
                           qb_t[:, col:col + 256], qb_h[:, col:col + 256], stat_sx[:, 4 + hh:5 + hh]),
                       reads=["qb.h", "stat_sx"], writes=["qb.t"])
            sch.op("vector", lambda v: v.tensor_mul(apre_ap, qb_t[:], nwa),
                   reads=["qb.t", nwk], writes=[apre_key])
            sch.op("sync", lambda s: s.dma_start(out=ap_s[c * 128:(c + 1) * 128, :],
                                                 in_=apre_ap),
                   reads=[apre_key], writes=["ap_s"], inc=16)

        return [l0, l1,
                mk_b(g64a, g64k, 512, dp2[:, UA:UA + 1024], "dp2.ua", qb_qh[:], "qb.qh"),
                mk_b(b23a, b23k, 0, qb_qh[:], "qb.qh", uc_ap, uc_key),
                mk_b(b23a, b23k, 512, uc_ap, uc_key, dp2[:, U0:U0 + 1024], "dp2.u0"),
                l5, l6, l7]

    _ORD = [(0, 0), (0, 1), (1, 0), (0, 2), (0, 3), (1, 1), (0, 4), (0, 5), (1, 2),
            (0, 6), (1, 3), (0, 7), (0, 8), (1, 4), (0, 9), (0, 10), (1, 5),
            (0, 11), (0, 12), (1, 6), (0, 13), (1, 7)]
    for f in front_stages(0):
        f()
    for c in range(NCH):
        fl = flow_stages(c)
        if c + 1 < NCH:
            fr = front_stages(c + 1)
            for which, idx in _ORD:
                (fr if which == 0 else fl)[idx]()
        else:
            for f in fl:
                f()

    sch.marks["B4"] = len(sch.entries)
    if "sfin_s" in debug_taps:
        sfin_s = scratch("sfin_s", [128, 1024], BF16)
        sch.op("sync", lambda sy: sy.dma_start(out=sfin_s[:], in_=Sbf[:]),
               reads=["Sbf"], writes=["sfin_s"], inc=16)

    NKC = I_FFN // 128
    wgT = [(qkk[:, f * 2816:(f + 1) * 2816], f"cg{f}") for f in range(8)]
    wuT = ([(h_sb[:, f * 2816:(f + 1) * 2816], f"cu{f}") for f in range(5)] +
           [(wvgo[:, (f - 5) * 2816:(f - 4) * 2816], f"cu{f}") for f in range(5, 8)])
    wdT = ([(wvgo[:, 9216 + j * 1024:9216 + (j + 1) * 1024], f"cd{j}") for j in range(7)] +
           [(wqk[:, (j - 7) * 1024:(j - 6) * 1024], f"cd{j}") for j in range(7, 15)] +
           [(dp[:, (j - 15) * 1024:(j - 14) * 1024], f"cd{j}") for j in range(15, 22)])
    CQ = ["sync", "scalar", "gpsimd"]
    cqi = [0]

    def cload(dst_ap, dkey, src_ap, skey, extra=(), q=None):
        if q is None:
            q = CQ[cqi[0] % 3]
            cqi[0] += 1
        sch.op(q, lambda s: s.dma_start(out=dst_ap, in_=src_ap),
               reads=[skey], writes=[dkey] + list(extra), inc=16)

    DPKEYS = ["dp.ag", "dp.p2", "dp.p4", "dp.p8", "dp.p16", "dp.p32", "dp.g64b1", "dp.b23"]
    for f in range(5):
        cload(wuT[f][0], wuT[f][1], wuqT_s[f * 128:(f + 1) * 128, :], "wuT",
              extra=["h"], q="gpsimd")
    for j in range(7):
        cload(wdT[j][0], wdT[j][1], wdqT_s[j * 128:(j + 1) * 128, :], "wdT",
              extra=["wg"], q="gpsimd")
    for j in range(15, 22):
        cload(wdT[j][0], wdT[j][1], wdqT_s[j * 128:(j + 1) * 128, :], "wdT",
              extra=DPKEYS, q="gpsimd")

    for kc in range(8):
        sch.op("gpsimd", lambda g, kc=kc: g.dma_start(out=wqk[:, kc * 1024:(kc + 1) * 1024],
                                                      in_=wo_in[kc * 128:(kc + 1) * 128, :]),
               writes=["wqk"], inc=16)
    apfm_b = [(wvgo[:, 0:4096], "wv"), (qkk[:, 2048:6144], "qkk")]

    def wo_prep(tb):
        apfm, apk = apfm_b[tb % 2]
        for sub in range(4):
            t = tb * 4 + sub
            lda = qkk[:, (sub % 2) * 1024:(sub % 2) * 1024 + 1024]
            sch.op("scalar", lambda s, t=t, lda=lda: s.dma_start(
                       out=lda, in_=ap_s[t * 128:(t + 1) * 128, :]),
                   reads=["ap_s", "qkk"], writes=["qkk"], inc=16)
            for f in range(8):
                sch.op("tensor", lambda tr, f=f, lda=lda: tr.transpose(
                           psT[:, f * 128:(f + 1) * 128], lda[:, f * 128:(f + 1) * 128],
                           idn_sb[:]),
                       reads=["qkk", "idn_sb"], writes=["psT"])
            for f in range(8):
                sch.op("scalar", lambda sc, f=f, sub=sub, apfm=apfm: sc.activation(
                           apfm[:, f * 512 + sub * 128:f * 512 + (sub + 1) * 128],
                           psT[:, f * 128:(f + 1) * 128], ACTF.Copy),
                       reads=["psT"], writes=[apk])

    wo_prep(0)
    for tb in range(4):
        apfm, apk = apfm_b[tb % 2]
        for sub in range(4):
            for g2 in range(2):
                for f in range(8):
                    mm(psF[:, g2 * 512:(g2 + 1) * 512],
                       apfm[:, f * 512 + sub * 128: f * 512 + (sub + 1) * 128],
                       wqk[:, f * 1024 + g2 * 512: f * 1024 + (g2 + 1) * 512],
                       [apk, "wqk"], "psF", start=(f == 0), stop=(f == 7))
            if sub == 0 and tb + 1 < 4:
                wo_prep(tb + 1)
            row = (tb * 4 + sub) * 128
            xa2, kx2 = (xt[0], "xt0") if sub % 2 == 0 else (xt[1], "xt1")
            ra2, kr2 = (xnt[0][:], "xnt0") if sub % 2 == 0 else (qb_t[:], "qb.t")
            sch.op("scalar", lambda s, row=row, xa2=xa2, kx2=kx2: s.dma_start(
                       out=xa2[:], in_=x_in[row:row + 128, :]),
                   reads=[kx2], writes=[kx2], inc=16)
            sch.op("vector", lambda v, xa2=xa2, ra2=ra2: v.tensor_add(ra2, xa2[:], psF[:]),
                   reads=[kx2, "psF"], writes=[kr2])
            sch.op("sync", lambda s, row=row, ra2=ra2, kr2=kr2: s.dma_start(
                       out=out_t[row:row + 128, :], in_=ra2),
                   reads=[kr2], writes=["out_partial"], inc=16)

    sch.barrier()

    sch.marks["B5"] = len(sch.entries)
    for f in range(8):
        cload(wgT[f][0], wgT[f][1], wgqT_s[f * 128:(f + 1) * 128, :], "wgT")
    for f in range(5, 8):
        cload(wuT[f][0], wuT[f][1], wuqT_s[f * 128:(f + 1) * 128, :], "wuT")
    for j in range(7, 15):
        cload(wdT[j][0], wdT[j][1], wdqT_s[j * 128:(j + 1) * 128, :], "wdT")

    xql = [(xnbt[0][:], "xnbt0"), (xnbt[1][:], "xnbt1")]
    xqTb = [(dp2[:, 0:1024], "cxq0"), (dp2[:, 1024:2048], "cxq1")]
    hqTb = [(dp2[:, 2048:2560], "chq0"), (dp2[:, 2560:3072], "chq1"),
            (dp2[:, 3072:3584], "chq2")]
    hqq = [(xqt[0][:, 0:512], "cq0"), (xqt[0][:, 512:1024], "cq1")]
    gsil_b = [(hpre[:, 0:512], "cgs0"), (hpre[:, 512:1024], "cgs1")]
    ht_b = [(hpre[:, 1024:1536], "cht0"), (hpre[:, 1536:2048], "cht1")]
    res_b = [(xt[0][:], "xt0", xnt[0][:], "xnt0"), (xt[1][:], "xt1", qb_h[:], "qb_h")]
    GRP = [(0, 512), (512, 512), (1024, 512), (1536, 512), (2048, 384), (2432, 384)]

    def c_xq_prep(tc):
        xa, kx = xql[tc % 2]
        qa, kq = xqTb[tc % 2]
        sch.op("scalar", lambda s: s.dma_start(out=xa, in_=xq_s[tc * 128:(tc + 1) * 128, :]),
               reads=["xq_s"], writes=[kx], inc=16)
        for j in range(8):
            sch.op("tensor", lambda t, j=j: t.transpose(psT[:, j * 128:(j + 1) * 128],
                                                        xa[:, j * 128:(j + 1) * 128], idn_sb[:]),
                   reads=[kx, "idn_sb"], writes=["psT"])
        sch.op("scalar", lambda s: s.activation(qa, psT[:], ACTF.Copy),
               reads=["psT"], writes=[kq])

    def c_gup(tc, g):
        gi0, gw = GRP[g]
        psu, psk = (psS, "psS") if g % 2 == 0 else (psV, "psV")
        qa, kq = xqTb[tc % 2]
        for f in range(8):
            mm(psu[:, 0:gw], qa[:, f * 128:(f + 1) * 128], wgT[f][0][:, gi0:gi0 + gw],
               [kq, wgT[f][1]], psk, start=(f == 0), stop=(f == 7))
        for f in range(8):
            mm(psu[:, 512:512 + gw], qa[:, f * 128:(f + 1) * 128], wuT[f][0][:, gi0:gi0 + gw],
               [kq, wuT[f][1]], psk, start=(f == 0), stop=(f == 7))

    def c_quant_a(tc, g):
        gi0, gw = GRP[g]
        par = g % 2
        psu, psk = (psS, "psS") if par == 0 else (psV, "psV")
        ga, kg = gsil_b[par]
        ha, kh = ht_b[par]
        sch.op("scalar", lambda s: s.activation(ga[:, 0:gw], psu[:, 0:gw], ACTF.Silu),
               reads=[psk], writes=[kg])
        sch.op("vector", lambda v: v.tensor_mul(ha[:, 0:gw], ga[:, 0:gw], psu[:, 512:512 + gw]),
               reads=[kg, psk], writes=[kh])
        quant_stage1(ha[:, 0:gw], kh, gw, QSETS_C[par])
        quant_stage2(ha[:, 0:gw], kh, gw, QSETS_C[par])

    def c_quant_b(tc, g):
        gi0, gw = GRP[g]
        par = g % 2
        qa2, kq2 = hqq[par]
        quant_stage3(gw, QSETS_C[par])
        quant_stage4(gw, QSETS_C[par])
        quant_stage5(qa2[:, 0:gw], kq2, gw, QSETS_C[par], "gpsimd")

    def c_tp(tc, g):
        gi0, gw = GRP[g]
        par = g % 2
        qa2, kq2 = hqq[par]
        ta, kt = hqTb[(tc * 6 + g) % 3]
        for j in range(gw // 128):
            sch.op("tensor", lambda t, j=j: t.transpose(psT[:, j * 128:(j + 1) * 128],
                                                        qa2[:, j * 128:(j + 1) * 128], idn_sb[:]),
                   reads=[kq2, "idn_sb"], writes=["psT"])
        sch.op("scalar", lambda s: s.activation(ta[:, 0:gw], psT[:, 0:gw], ACTF.Copy),
               reads=["psT"], writes=[kt])

    def c_dmm(tc, g):
        gi0, gw = GRP[g]
        ta, kt = hqTb[(tc * 6 + g) % 3]
        for j in range(gw // 128):
            kc = gi0 // 128 + j
            wda, kwd = wdT[kc]
            mm(psF[:, 0:512], ta[:, j * 128:(j + 1) * 128], wda[:, 0:512],
               [kt, kwd], "psF", start=(kc == 0), stop=(kc == NKC - 1))
            mm(psF[:, 512:1024], ta[:, j * 128:(j + 1) * 128], wda[:, 512:1024],
               [kt, kwd], "psF", start=(kc == 0), stop=(kc == NKC - 1))

    def c_res(tc):
        xa, kx, ra, kr = res_b[tc % 2]
        sch.op("scalar", lambda s: s.dma_start(out=xa, in_=out_t[tc * 128:(tc + 1) * 128, :]),
               reads=["out_partial"], writes=[kx], inc=16)
        sch.op("vector", lambda v: v.tensor_add(ra, xa, psF[:]),
               reads=[kx, "psF"], writes=[kr])
        sch.op("sync", lambda s: s.dma_start(out=out_t[tc * 128:(tc + 1) * 128, :], in_=ra),
               reads=[kr], writes=["out_partial"], inc=16)

    NGR = NT * 6
    c_xq_prep(0)
    for m in range(NGR + 4):
        if 0 <= m - 2 < NGR:
            tcd, gd = divmod(m - 2, 6)
            c_tp(tcd, gd)
        if 0 <= m - 1 < NGR:
            tcq, gq = divmod(m - 1, 6)
            c_quant_b(tcq, gq)
        if m < NGR:
            tc, g = divmod(m, 6)
            if g == 4 and tc + 1 < NT:
                c_xq_prep(tc + 1)
            c_gup(tc, g)
            c_quant_a(tc, g)
        if 0 <= m - 4 < NGR:
            tcd, gd = divmod(m - 4, 6)
            c_dmm(tcd, gd)
            if gd == 5:
                c_res(tcd)

    sch.barrier()
    nc_sched = (nc, sch)
    return nc_sched, locals()


def finish_program(nc, sch, max_entries=None):
    sems = {}
    ctxs = []
    for e in ENGINES:
        c = nc.semaphore(f"sem_{e}")
        h = c.__enter__()
        ctxs.append(c)
        sems[(e, "c")] = h
    for e in DMAQ:
        c = nc.semaphore(f"semd_{e}")
        h = c.__enter__()
        ctxs.append(c)
        sems[(e, "d")] = h
    blk_ctx = nc.Block()
    blk = blk_ctx.__enter__()

    blk.sync(lambda h: sch.replay("sync", h, sems, max_entries))
    blk.scalar(lambda h: sch.replay("scalar", h, sems, max_entries))
    blk.vector(lambda h: sch.replay("vector", h, sems, max_entries))
    blk.tensor(lambda h: sch.replay("tensor", h, sems, max_entries))
    blk.gpsimd(lambda h: sch.replay("gpsimd", h, sems, max_entries))

    blk_ctx.__exit__(None, None, None)
    for c in reversed(ctxs):
        c.__exit__(None, None, None)
    mybir.codegen_inst_isa_subclasses(nc)
    return nc


def make_inputs_per_core(core, full):
    (x, convw, wq, wk, wv, wb, wg, nw, wo, wgate, wup, wdown) = full
    d = {
        "x": np.ascontiguousarray(x[core]),
        "convw": np.ascontiguousarray(convw.reshape(8, 128, 4).transpose(1, 0, 2).reshape(128, 32)),
        "wq": wq, "wk": wk, "wv": wv, "wg": wg, "wo": wo,
        "wbt": np.ascontiguousarray(wb.reshape(8, 128, 4).transpose(1, 0, 2).reshape(128, 32)),
        "wgate": wgate, "wup": wup, "wdown": wdown,
        "nwrep4": np.ascontiguousarray(np.tile(nw[None, :], (128, 4)).astype(np.float32)),
        "identb": np.eye(128, dtype=ml_dtypes.bfloat16),
        "onesb": np.ones((128, 128), dtype=ml_dtypes.bfloat16),
        "maskag": _mask_ag(),
        "maskaq": _mask_aq(),
        "selh": _selh(),
    }
    return d


def _mask_ag():
    sl = np.tril(np.ones((128, 128), np.float32), -1)
    su = np.triu(np.ones((128, 128), np.float32), 1)
    m = np.concatenate([sl, su] * 4, axis=1)[:, :1024]
    return np.ascontiguousarray(m.astype(ml_dtypes.bfloat16))


def _selh():
    m = np.zeros((4, 512), np.float32)
    for hh in range(4):
        m[hh, hh * 128:(hh + 1) * 128] = 1.0
    return np.ascontiguousarray(m.astype(ml_dtypes.bfloat16))


def _mask_aq():
    su = np.triu(np.ones((128, 128), np.float32), 0)
    return np.ascontiguousarray(np.concatenate([su] * 4, axis=1).astype(ml_dtypes.bfloat16))


def kernel(L0_attn_hidden_states, L0_attn_conv_w, L0_attn_Wq, L0_attn_Wk,
           L0_attn_Wv, L0_attn_Wb, L0_attn_Wg, L0_attn_norm_w, L0_attn_Wo,
           L0_ffn_w_gate, L0_ffn_w_up, L0_ffn_w_down):
    full = (np.asarray(L0_attn_hidden_states, np.float32),
            np.asarray(L0_attn_conv_w, np.float32),
            np.asarray(L0_attn_Wq, np.float32),
            np.asarray(L0_attn_Wk, np.float32),
            np.asarray(L0_attn_Wv, np.float32),
            np.asarray(L0_attn_Wb, np.float32),
            np.asarray(L0_attn_Wg, np.float32),
            np.asarray(L0_attn_norm_w, np.float32),
            np.asarray(L0_attn_Wo, np.float32),
            np.asarray(L0_ffn_w_gate, np.float32),
            np.asarray(L0_ffn_w_up, np.float32),
            np.asarray(L0_ffn_w_down, np.float32))
    (nc, sch), env = build_core_program()
    finish_program(nc, sch)
    in_maps = [make_inputs_per_core(c, full) for c in range(8)]
    res = run_bass_kernel_spmd(nc, in_maps, core_ids=list(range(8)))
    out = np.stack([r["out"] for r in res.results], axis=0)
    return out
